# version 39
# speedup vs baseline: 1.1063x; 1.0090x over previous
import math

import numpy as np
import ml_dtypes
import concourse.bass as bass
import concourse.mybir as mybir
from concourse.bass_utils import run_bass_kernel_spmd

F32 = mybir.dt.float32
BF16 = mybir.dt.bfloat16
ALU = mybir.AluOpType
AF = mybir.ActivationFunctionType
AXX = mybir.AxisListType.X

C = 2048
NF = 16
SUB = 16
WIN = 64
LN_EPS = 1e-5
LN2 = math.log(2.0)
RNE_C = 1.5 * 2.0 ** 23
QC_LN = math.log(1.5 * 2.0 ** 20)
FP8_MAX = 448.0
SCALE_EPS = 1e-7
_DTSZ = {F32: 4, BF16: 2}

VORD = ["maa_x", "maa_r", "maa_k", "maa_v", "maa_w", "td", "q_b", "k_b",
        "v_b", "ln_g", "ln_b", "f_b"]
VEC = {n: i for i, n in enumerate(VORD)}


class Em:

    def __init__(self, nc, needed=None):
        self.nc = nc
        self._sems = {}
        self.ctr = {}
        self.hw = {}
        self.bufs = {}
        self.streams = {
            "pe": (nc.tensor, "pe"),
            "dve": (nc.vector, "dve"),
            "act": (nc.scalar, "act"),
            "pool": (nc.gpsimd, "pool"),
            "sp": (nc.sync, "sp"),
        }
        self._rr = 0
        self.needed = needed
        self.waited_ids = set()
        self._opid = 0
        self._tok2id = {}

    def eng(self, pool_ok=True):
        self._rr += 1
        return ("dve", "pool")[self._rr % 2] if pool_ok else "dve"

    def sem(self, name):
        if name not in self._sems:
            self._sems[name] = self.nc.alloc_semaphore(name)
            self.ctr[name] = 0
        return self._sems[name]

    def _emit_waits(self, stream, toks):
        eng, key = self.streams[stream]
        hwm = self.hw.setdefault(key, {})
        best = {}
        for t in toks:
            if t is None:
                continue
            s, v = t
            if key == "pe" and s == "P_pe":
                continue
            if best.get(s, -1) < v:
                best[s] = v
        for s, v in sorted(best.items()):
            if hwm.get(s, 0) >= v:
                continue
            eng.wait_ge(self.sem(s), v)
            hwm[s] = v
            if self.needed is None:
                oid = self._tok2id.get((s, v))
                if oid is not None:
                    self.waited_ids.add(oid)

    def _buf(self, name):
        return self.bufs.setdefault(name, {"w": [], "r": []})

    def _deps(self, reads, writes):
        toks = []
        for b in reads:
            toks.extend(self._buf(b)["w"])
        for b in writes:
            d = self._buf(b)
            toks.extend(d["w"])
            toks.extend(d["r"])
        return toks

    def _finish(self, tok, reads, writes):
        for b in writes:
            self.bufs[b] = {"w": [tok], "r": []}
        for b in reads:
            d = self._buf(b)
            d["r"].append(tok)
            if len(d["r"]) > 96:
                d["r"] = d["r"][-96:]

    def op(self, stream, fn, reads=(), writes=(), waits=()):
        eng, key = self.streams[stream]
        self._emit_waits(stream, self._deps(reads, writes) + list(waits))
        inst = fn(eng)
        semn = "P_" + key
        self.sem(semn)
        self._opid += 1
        if self.needed is None or self._opid in self.needed:
            self.ctr[semn] += 1
            inst.then_inc(self._sems[semn], 1)
        tok = (semn, self.ctr[semn])
        if self.needed is None:
            self._tok2id[tok] = self._opid
        self._finish(tok, reads, writes)
        return tok

    def dma(self, stream, dsem, out, in_, reads=(), writes=(), waits=(), **kw):
        eng, key = self.streams[stream]
        self.sem(dsem)
        toks = self._deps(reads, writes) + list(waits)
        if self.ctr[dsem] > 0:
            toks.append((dsem, self.ctr[dsem]))
        self._emit_waits(stream, toks)
        inst = eng.dma_start(out=out, in_=in_, **kw)
        self.sem(dsem)
        self.ctr[dsem] += 16
        inst.then_inc(self._sems[dsem], 16)
        tok = (dsem, self.ctr[dsem])
        self._finish(tok, reads, writes)
        return tok

    def collective(self, kind, op, groups, ins, outs, reads=(), writes=()):
        eng, key = self.streams["pool"]
        self._emit_waits("pool", self._deps(reads, writes))
        semn = "cc%d" % len([s for s in self._sems if s.startswith("cc")])
        inst = eng.collective_compute(kind, op, replica_groups=groups,
                                     ins=ins, outs=outs)
        inst.then_inc(self.sem(semn))
        self.ctr[semn] = 1
        tok = (semn, 1)
        self._finish(tok, reads, writes)
        return tok

    def barrier(self, streams=("pe", "dve", "act", "pool", "sp")):
        toks = []
        for s in ("pe", "dve", "act", "pool"):
            semn = "P_" + s
            if semn in self.ctr:
                toks.append((semn, self.ctr[semn]))
        for name, d in self.bufs.items():
            toks.extend(d["w"])
            toks.extend(d["r"])
        toks = [t for t in toks if t is not None]
        for s, v in list(self.ctr.items()):
            if not s.startswith("P_"):
                toks.append((s, v))
        for s in streams:
            self._emit_waits(s, toks)


class Arena:

    def __init__(self, nc, part_bytes):
        self.guard = nc.sbuf_tensor("arena", [128, part_bytes // 4], F32)
        self.t = self.guard.__enter__()
        self.cap = part_bytes
        self.free_list = [(0, part_bytes)]
        self.live = {}

    def alloc(self, name, parts, nelem, dtype):
        assert name not in self.live, name
        nb = ((nelem * _DTSZ[dtype] + 63) // 64) * 64
        big = nb >= 16384
        fits = [(off, i) for i, (off, sz) in enumerate(self.free_list)
                if sz >= nb]
        if fits:
            _, i = max(fits) if big else min(fits)
            off, sz = self.free_list[i]
            if sz == nb:
                self.free_list.pop(i)
            elif big:
                self.free_list[i] = (off, sz - nb)
                off = off + sz - nb
            else:
                self.free_list[i] = (off + nb, sz - nb)
            self.live[name] = (off, nb)
            o4 = off // 4
            ap = self.t.ap()[0:parts, o4:o4 + nb // 4]
            if dtype != F32:
                ap = ap.bitcast(dtype)
            return ap[:, 0:nelem]
        raise MemoryError(f"arena OOM for {name} ({nb}B); free={self.free_list}")

    def free(self, *names):
        for name in names:
            off, nb = self.live.pop(name)
            self.free_list.append((off, nb))
        self.free_list.sort()
        merged = []
        for off, sz in self.free_list:
            if merged and merged[-1][0] + merged[-1][1] == off:
                merged[-1] = (merged[-1][0], merged[-1][1] + sz)
            else:
                merged.append((off, sz))
        self.free_list = [tuple(x) for x in merged]


def rep(ap2d, nrep, plen):
    P = ap2d.shape[0]
    return ap2d.rearrange("p (o f) -> p o f", o=1).broadcast_to([P, nrep, plen])


def build_program(t_core, n_cores, debug=False):
    b1 = Builder(t_core, n_cores, debug)
    b1.build()
    b2 = Builder(t_core, n_cores, debug, needed=b1.em.waited_ids)
    return b2.build()


class Builder:
    def __init__(self, t_core, n_cores, debug=False, needed=None):
        assert t_core % 128 == 0
        self.NT = t_core
        self.NH = t_core // 2
        self.NTB = t_core // 128
        self.NW = t_core // WIN
        self.THS = min(512, t_core)
        self.NTH = t_core // self.THS
        self.n_cores = n_cores
        self.debug = debug
        self.nc = bass.Bass("TRN2", target_bir_lowering=False, debug=False)
        self.em = Em(self.nc, needed=needed)
        self.ar = Arena(self.nc, 206 * 1024)
        self._psum_stack = []
        self._psum_live = {}
        self.dbg = {}

    def op(self, *a, **k):
        return self.em.op(*a, **k)

    def dma(self, *a, **k):
        return self.em.dma(*a, **k)

    def psum(self, name, shape, dtype=F32):
        self._psum_seq = getattr(self, "_psum_seq", 0) + 1
        elems_2k = 2048 // _DTSZ[dtype]
        nfree = ((shape[1] + elems_2k - 1) // elems_2k) * elems_2k
        g = self.nc.psum_tensor(f"{name}_{self._psum_seq}", [128, nfree], dtype)
        t = g.__enter__()
        self._psum_stack.append(name)
        self._psum_live[name] = g
        return t.ap()[0:shape[0], 0:shape[1]]

    def psum_free(self, *names):
        names = sorted(names, key=self._psum_stack.index, reverse=True)
        for n in names:
            assert self._psum_stack[-1] == n, (n, self._psum_stack)
            self._psum_stack.pop()
            self._psum_live.pop(n).__exit__(None, None, None)

    def vc(self, name, p):
        i = VEC[name] * 16 + p
        return self.vec_t[:, i:i + 1]

    def build(self):
        nc, em, ar = self.nc, self.em, self.ar
        NT, NH, NTB, NW, THS, NTH = (self.NT, self.NH, self.NTB, self.NW,
                                     self.THS, self.NTH)
        din = lambda n, s, dt=F32: nc.dram_tensor(n, s, dt,
                                                  kind="ExternalInput").ap()
        self.x_in = din("x_in", [NT, C])
        self.xprev = din("xprev", [1, C])
        self.q_w = din("q_w", [C, C]); self.k_w = din("k_w", [C, C])
        self.v_w = din("v_w", [C, C]); self.o_w = din("o_w", [C, C])
        self.ffn_wT = din("ffn_wT", [C, C])
        self.maa_w1 = din("maa_w1", [C, 128])
        self.w2pad = din("w2pad", [4 * 128, C])
        self.td_w1 = din("td_w1", [C, 64])
        self.td_w2 = din("td_w2", [64, C])
        self.vecs = din("vecs", [128, NF * len(VORD)])
        self.vbx = din("vbx", [1, C])
        self.ffn_ws = din("ffn_ws", [1, 1])
        self.ident_i = din("ident", [128, 128], BF16)
        self.ones_i = din("ones", [128, 128], BF16)
        self.m_i = {}
        for nm in ("mA16", "mB16", "mE16", "nE16", "mAw", "mBw", "mEw", "nEw"):
            plen = 16 if nm.endswith("16") else WIN
            self.m_i[nm] = din(nm, [128, plen], BF16)
        self.diagm_i = din("diagm", [128, WIN], BF16)
        self.adjm_i = din("adjm", [128, 48], BF16)
        self.g16m_i = din("g16m", [128, 32], BF16)
        self.cbias_i = din("cbias", [128, 4])
        self.cmask_i = din("cmask", [128, 8])
        self.out_t = nc.dram_tensor("out", [NT, C], F32,
                                    kind="ExternalOutput").ap()
        if self.debug:
            for nm, shp in [("d_h", [128, NF * NT]), ("d_q", [128, NF * NT]),
                            ("d_k", [128, NF * NT]), ("d_g", [128, NF * NT]),
                            ("d_v", [128, NTB * C]), ("d_y", [128, NF * NT]),
                            ("d_x1", [128, NF * NT]), ("d_xq", [128, NF * NT]),
                            ("d_sin", [128, C])]:
                self.dbg[nm] = nc.dram_tensor(nm, shp, F32,
                                              kind="ExternalOutput").ap()
        self.x0scr = nc.dram_tensor("x0scr", [NT, C], BF16).ap()
        self.outscr = nc.dram_tensor("outscr", [C, NT], BF16).ap()
        self.qgscr = nc.dram_tensor("qgscr", [C, NT], BF16).ap()
        self.qscr = nc.dram_tensor("qscr", [C, NT], BF16).ap()
        self.kscr = nc.dram_tensor("kscr", [C, NT], BF16).ap()
        self.vscr = nc.dram_tensor("vscr", [128, self.NTB * C], BF16).ap()
        self.x1scr = nc.dram_tensor("x1scr", [128, NF * NT], BF16).ap()
        self.cc_in0 = nc.dram_tensor("cc_in0", [128, C // 2], F32).ap()
        self.cc_out0 = nc.dram_tensor("cc_out0", [self.n_cores * 128, C // 2],
                                      F32).ap()
        self.cc_inA = nc.dram_tensor("cc_inA", [128, C // 4], F32).ap()
        self.cc_outA = nc.dram_tensor(
            "cc_outA", [self.n_cores * 128, C // 4], F32).ap()
        self.cc_inB = nc.dram_tensor("cc_inB", [128, C // 4], F32).ap()
        self.cc_outB = nc.dram_tensor(
            "cc_outB", [self.n_cores * 128, C // 4], F32).ap()
        self.cc_inD = nc.dram_tensor("cc_inD", [128, 16], F32).ap()
        self.cc_outD = nc.dram_tensor(
            "cc_outD", [self.n_cores * 128, 16], F32).ap()
        self.mx_in = nc.dram_tensor("mx_in", [1, 4], F32).ap()
        self.mx_out = nc.dram_tensor("mx_out", [1, 4], F32).ap()
        self.mxscr = nc.dram_tensor("mxscr", [128, 1], F32).ap()
        self.hrowscr = nc.dram_tensor("hrowscr", [1, C], BF16).ap()

        with nc.Block():
            self.ph0_setup()
            self.ph1_h()
            self.ph1_mix()
            self.ph1_proj()
            self.ph1_rec()
            self.ph15_cc()
            self.ph2()
        return nc

    def ph0_setup(self):
        em, ar = self.em, self.ar
        NT = self.NT
        A = ar.alloc
        self.vec_t = A("vec_t", 128, NF * len(VORD), F32)
        self.ident = A("ident", 128, 128, BF16)
        self.ones = A("ones", 128, 128, BF16)
        self.masks = {nm: A(nm, 128, 16 if nm.endswith("16") else WIN, BF16)
                      for nm in self.m_i}
        self.diagm = A("diagm", 128, WIN, BF16)
        self.adjm = A("adjm", 128, 48, BF16)
        self.g16m = A("g16m", 128, 32, BF16)
        self.cmask = A("cmask", 128, 8, F32)
        self.cbias = A("cbias", 128, 4, F32)
        self.qb8 = A("qb8", 128, 16, F32)
        self.zcol = A("zcol", 128, 1, BF16)
        self.S_all = A("S_all", 128, NF * 128, F32)
        self.Dtotb = A("Dtotb", 128, 16, F32)
        self.ubuf = A("ubuf", 64, NT, BF16)
        self.vbb = A("vbb", 128, C, BF16)
        self.wsl = A("wsl", 128, 1, F32)
        self.w2cur = A("w2cur", 128, C, BF16)
        self.tdw2b = A("tdw2b", 64, C, BF16)
        self.mmt = A("mmt", 128, NT, BF16)
        self.mtf = [A("mtf0", 128, NT // 2, F32), A("mtf1", 128, NT // 2, F32)]
        self.mtb = [A("mtb0", 128, NT // 2, BF16),
                    A("mtb1", 128, NT // 2, BF16)]
        self.wld = [A(f"wld{i}", 128, 512, F32) for i in range(4)]
        self.wbf = [A(f"wbf{i}", 128, 512, BF16) for i in range(4)]
        self.hrowf = A("hrowf", 1, C, F32)
        self.hrowb = A("hrowb", 1, C, BF16)
        self.hrsq = A("hrsq", 1, C, F32)
        self.tiny = A("tiny", 1, 16, F32)

        em.dma("sp", "d_cst", self.vec_t, self.vecs[:], writes=("vec_t",))
        em.dma("sp", "d_cst", self.ident, self.ident_i[:], writes=("ident",))
        em.dma("sp", "d_cst", self.ones, self.ones_i[:], writes=("ones",))
        for nm in self.m_i:
            em.dma("sp", "d_cst", self.masks[nm], self.m_i[nm][:], writes=(nm,))
        em.dma("sp", "d_cst", self.diagm, self.diagm_i[:], writes=("diagm",))
        em.dma("sp", "d_cst", self.adjm, self.adjm_i[:], writes=("adjm",))
        em.dma("sp", "d_cst", self.g16m, self.g16m_i[:], writes=("g16m",))
        em.dma("sp", "d_cst", self.cmask, self.cmask_i[:], writes=("cmask",))
        em.dma("sp", "d_cst", self.cbias, self.cbias_i[:], writes=("cbias",))
        for tb in range(self.NTB):
            em.dma("pool", f"d_x0c{tb % 4}",
                   self.x0scr[tb * 128:(tb + 1) * 128, :],
                   self.x_in[tb * 128:(tb + 1) * 128, :],
                   writes=("x0scr",))
        em.dma("sp", "d_hrow", self.hrowf, self.xprev[:], writes=("hrowf",))

        em.op("dve", lambda e: e.memset(self.zcol, 0.0), writes=("zcol",))
        em.op("dve", lambda e: e.memset(self.S_all, 0.0), writes=("S_all",))
        em.op("dve", lambda e: e.tensor_scalar(
            self.qb8, self.vec_t[:, VEC["q_b"] * 16:(VEC["q_b"] + 1) * 16],
            0.125, None, ALU.mult), reads=("vec_t",), writes=("qb8",))

        ws_t = ar.alloc("ws_t", 1, 1, F32)
        wsb = ar.alloc("wsb", 1, 1, BF16)
        vbr = ar.alloc("vbr", 1, C, BF16)
        ps_b = self.psum("ps_b", [128, 512], F32)
        em.dma("sp", "d_cst", ws_t, self.ffn_ws[:], writes=("ws_t",))
        em.op("dve", lambda e: e.tensor_copy(wsb, ws_t), reads=("ws_t",),
              writes=("wsb",))
        em.op("pe", lambda e: e.matmul(ps_b[:, 0:1], self.ones[0:1, :], wsb,
                                       start=True, stop=True),
              reads=("ones", "wsb"), writes=("ps_b",))
        em.op("dve", lambda e: e.tensor_copy(self.wsl, ps_b[:, 0:1]),
              reads=("ps_b",), writes=("wsl",))
        em.dma("pool", "d_cstp", vbr, self.vbx[:], writes=("vbr",))
        for j in range(C // 512):
            em.op("pe", lambda e, j=j: e.matmul(
                ps_b[:, 0:512], self.ones[0:1, :], vbr[:, j * 512:(j + 1) * 512],
                start=True, stop=True), reads=("ones", "vbr"), writes=("ps_b",))
            em.op("dve", lambda e, j=j: e.tensor_copy(
                self.vbb[:, j * 512:(j + 1) * 512], ps_b[:, 0:512]),
                reads=("ps_b",), writes=("vbb",))
        self.psum_free("ps_b")

        t = self.tiny
        em.op("dve", lambda e: e.tensor_reduce(t[0:1, 0:1], self.hrowf, AXX,
                                               ALU.add),
              reads=("hrowf",), writes=("tiny",))
        em.op("dve", lambda e: e.tensor_scalar(t[0:1, 1:2], t[0:1, 0:1],
                                               1.0 / C, None, ALU.mult),
              reads=("tiny",), writes=("tiny",))
        em.op("dve", lambda e: e.tensor_tensor(self.hrsq, self.hrowf,
                                               self.hrowf, ALU.mult),
              reads=("hrowf",), writes=("hrsq",))
        em.op("dve", lambda e: e.tensor_reduce(t[0:1, 2:3], self.hrsq, AXX,
                                               ALU.add),
              reads=("hrsq",), writes=("tiny",))
        em.op("dve", lambda e: e.tensor_scalar(t[0:1, 3:4], t[0:1, 2:3],
                                               1.0 / C, None, ALU.mult),
              reads=("tiny",), writes=("tiny",))
        em.op("dve", lambda e: e.tensor_tensor(t[0:1, 4:5], t[0:1, 1:2],
                                               t[0:1, 1:2], ALU.mult),
              reads=("tiny",), writes=("tiny",))
        em.op("dve", lambda e: e.tensor_tensor(t[0:1, 5:6], t[0:1, 3:4],
                                               t[0:1, 4:5], ALU.subtract),
              reads=("tiny",), writes=("tiny",))
        em.op("act", lambda e: e.activation(t[0:1, 6:7], t[0:1, 5:6], AF.Ln,
                                            bias=self.cbias[0:1, 0:1]),
              reads=("tiny", "cbias"), writes=("tiny",))
        em.op("act", lambda e: e.activation(t[0:1, 7:8], t[0:1, 6:7], AF.Exp,
                                            scale=-0.5),
              reads=("tiny",), writes=("tiny",))
        em.op("dve", lambda e: e.tensor_scalar(self.hrowf, self.hrowf,
                                               t[0:1, 1:2], None, ALU.subtract),
              reads=("tiny",), writes=("hrowf",))
        em.op("dve", lambda e: e.tensor_scalar(self.hrowb, self.hrowf,
                                               t[0:1, 7:8], None, ALU.mult),
              reads=("hrowf", "tiny"), writes=("hrowb",))
        em.dma("pool", "d_hcol", self.hrowscr[:], self.hrowb[0:1, :],
               reads=("hrowb",), writes=("hrowscr",))
        em.barrier()
        ar.free("hrowf", "hrowb", "hrsq", "tiny", "ws_t", "wsb", "vbr")

    def ln_open(self):
        A = self.ar.alloc
        NT = self.NT
        self.stat1 = A("stat1", 128, NT, F32)
        self.stat2 = A("stat2", 128, NT, F32)
        self.stat3 = A("stat3", 128, NT, F32)
        self.stat4 = A("stat4", 128, NT, F32)
        self.sq0 = A("sq0", 128, NT, BF16)
        self.sq1 = A("sq1", 128, NT, BF16)

    def ln_close(self):
        self.em.barrier()
        self.ar.free("stat1", "stat2", "stat3", "stat4", "sq0", "sq1")

    def ln_stats(self, src_name, src_fn, ps_sum, ps_sq, pssum_n, pssq_n):
        em = self.em
        NT, THS, NTH = self.NT, self.THS, self.NTH
        for p in range(NF):
            sq = (self.sq0, self.sq1)[p % 2]
            sqn = ("sq0", "sq1")[p % 2]
            s = src_fn(p)
            em.op(em.eng(), lambda e, s=s, sq=sq: e.tensor_tensor(
                sq, s, s, ALU.mult), reads=(src_name,), writes=(sqn,))
            for th in range(NTH):
                sl = slice(th * THS, th * THS + THS)
                em.op("pe", lambda e, s=s, sl=sl, p=p: e.matmul(
                    ps_sum[:, sl], self.ones, s[:, sl],
                    start=(p == 0), stop=(p == NF - 1)),
                    reads=(src_name, "ones"), writes=(pssum_n,))
                em.op("pe", lambda e, sq=sq, sl=sl, p=p: e.matmul(
                    ps_sq[:, sl], self.ones, sq[:, sl],
                    start=(p == 0), stop=(p == NF - 1)),
                    reads=(sqn, "ones"), writes=(pssq_n,))
        em.op("dve", lambda e: e.tensor_scalar(self.stat1, ps_sum[:], 1.0 / C,
                                               None, ALU.mult),
              reads=(pssum_n,), writes=("stat1",))
        em.op("dve", lambda e: e.tensor_tensor(self.stat3, self.stat1,
                                               self.stat1, ALU.mult),
              reads=("stat1",), writes=("stat3",))
        em.op("dve", lambda e: e.scalar_tensor_tensor(
            self.stat2, ps_sq[:], 1.0 / C, self.stat3, ALU.mult, ALU.subtract),
            reads=(pssq_n, "stat3"), writes=("stat2",))
        em.op("act", lambda e: e.activation(self.stat2, self.stat2, AF.Ln,
                                            bias=self.cbias[:, 0:1]),
              reads=("stat2", "cbias"), writes=("stat2",))
        em.op("act", lambda e: e.activation(self.stat2, self.stat2, AF.Exp,
                                            scale=-0.5),
              reads=("stat2",), writes=("stat2",))

    def ph1_h(self):
        em, ar = self.em, self.ar
        NT = self.NT
        self.ln_open()
        self.x0fm = ar.alloc("x0fm", 128, NF * NT, BF16)
        self.hbuf = ar.alloc("hbuf", 128, NF * (NT + 1), BF16)
        hcolt = ar.alloc("hcolt", 128, NF, BF16)
        for p in range(4):
            em.dma("sp", f"d_x0t{p % 4}", self.x0fm[:, p * NT:(p + 1) * NT],
                   self.x0scr[:, p * 128:(p + 1) * 128], reads=("x0scr",),
                   writes=("x0fm",), transpose=True)
        em.dma("sp", "d_hct", hcolt,
               self.hrowscr.rearrange("o (f p) -> (o f) p", p=128),
               reads=("hrowscr",), writes=("hcolt",), transpose=True)
        for p in range(4, NF):
            em.dma("sp", f"d_x0t{p % 4}", self.x0fm[:, p * NT:(p + 1) * NT],
                   self.x0scr[:, p * 128:(p + 1) * 128], reads=("x0scr",),
                   writes=("x0fm",), transpose=True)
        em.op("dve", lambda e: e.tensor_copy(
            self.hbuf.rearrange("p (f t) -> p f t", t=NT + 1)[:, :, 0:1],
            hcolt.rearrange("p (f o) -> p f o", o=1)),
            reads=("hcolt",), writes=("hbuf",))
        ps_sum = self.psum("ps_sum", [128, NT], F32)
        ps_sq = self.psum("ps_sq", [128, NT], F32)
        self.ln_stats("x0fm", lambda p: self.x0fm[:, p * NT:(p + 1) * NT],
                      ps_sum, ps_sq, "ps_sum", "ps_sq")
        for p in range(NF):
            hsl = self.hbuf[:, p * (NT + 1) + 1:p * (NT + 1) + 1 + NT]
            st, sn = ((self.stat3, "stat3"), (self.stat4, "stat4"))[p % 2]
            eng = "pool" if p % 4 == 3 else "dve"
            em.op(eng, lambda e, p=p, st=st: e.tensor_tensor(
                st, self.x0fm[:, p * NT:(p + 1) * NT], self.stat1,
                ALU.subtract), reads=("x0fm", "stat1"), writes=(sn,))
            em.op(eng, lambda e, hsl=hsl, st=st: e.tensor_tensor(
                hsl, st, self.stat2, ALU.mult),
                reads=(sn, "stat2"), writes=("hbuf",))
        self.psum_free("ps_sum", "ps_sq")
        if self.debug:
            self._dump_fm("d_h", lambda p: self.hbuf[
                :, p * (NT + 1) + 1:p * (NT + 1) + 1 + NT])
        em.barrier()
        ar.free("x0fm", "hcolt")
        self.ln_close()

    def _dump_fm(self, name, src_fn, nchunks=NF, src_names=()):
        em, ar = self.em, self.ar
        NT = self.NT
        em.barrier()
        try:
            d = ar.alloc("dump", 128, nchunks * NT, F32)
        except MemoryError:
            return
        for p in range(nchunks):
            em.op("dve", lambda e, p=p: e.tensor_copy(
                d[:, p * NT:(p + 1) * NT], src_fn(p)),
                reads=tuple(src_names), writes=("dump",))
        em.dma("pool", "d_dbg", self.dbg[name][:], d, reads=("dump",))
        em.barrier()
        ar.free("dump")

    def ph1_mix(self):
        em, ar = self.em, self.ar
        NT, NH = self.NT, self.NH
        A = ar.alloc
        self.w1b = A("w1b", 128, NF * 128, BF16)
        self.tdw1b = A("tdw1b", 128, NF * 64, BF16)
        em.dma("pool", "d_wsm",
               self.w1b.rearrange("p (a c) -> p a c", c=128),
               self.maa_w1.rearrange("(a p) c -> p a c", p=128),
               writes=("w1b",))
        em.dma("pool", "d_wsm",
               self.tdw1b.rearrange("p (a c) -> p a c", c=64),
               self.td_w1.rearrange("(a p) c -> p a c", p=128),
               writes=("tdw1b",))
        em.dma("pool", "d_wsm", self.tdw2b, self.td_w2[:, :],
               writes=("tdw2b",))

        self.dxf = A("dxf", 128, NF * NT, BF16)
        xxh = A("xxh", 128, NF * NH, BF16)
        m_ps = [self.psum("m_ps0", [128, NH], F32),
                self.psum("m_ps1", [128, NH], F32)]
        m_psr = [self.psum("m_psr0", [128, NH], F32),
                 self.psum("m_psr1", [128, NH], F32)]
        mm_ps = self.psum("mm_ps", [128, NH], F32)
        u_ps = self.psum("u_ps", [64, NH], F32)
        w2r = A("w2r", 128, NF * 128, BF16)
        em.dma("pool", "d_wsm2", w2r, self.w2pad[0:128, :], writes=("w2r",))
        self.xch = {}
        self.xch["r"] = A("xchr", 128, NF * NT, BF16)
        xwh = A("xwh", 128, NF * NH, BF16)
        mtf, mtb = self.mtf, self.mtb

        def pass1_thunks(hf):
            ho = hf * NH
            th_l = []
            E = th_l.append
            for p in range(NF):
                hb = p * (NT + 1) + ho
                E(lambda p=p, hb=hb, ho=ho: em.op(
                    em.eng(), lambda e: e.tensor_tensor(
                        self.dxf[:, p * NT + ho:p * NT + ho + NH],
                        self.hbuf[:, hb:hb + NH],
                        self.hbuf[:, hb + 1:hb + 1 + NH],
                        ALU.subtract), reads=("hbuf",), writes=("dxf",)))
            for p in range(NF):
                hb = p * (NT + 1) + ho + 1
                E(lambda p=p, hb=hb, ho=ho: em.op(
                    "dve", lambda e: e.scalar_tensor_tensor(
                        xxh[:, p * NH:(p + 1) * NH],
                        self.dxf[:, p * NT + ho:p * NT + ho + NH],
                        self.vc("maa_x", p), self.hbuf[:, hb:hb + NH],
                        ALU.mult, ALU.add),
                    reads=("dxf", "hbuf", "vec_t"), writes=("xxh",)))
            for kc in range(NF):
                E(lambda kc=kc: em.op("pe", lambda e: e.matmul(
                    mm_ps[:], self.w1b[:, kc * 128:(kc + 1) * 128],
                    xxh[:, kc * NH:(kc + 1) * NH],
                    start=(kc == 0), stop=(kc == NF - 1)),
                    reads=("w1b", "xxh"), writes=("mm_ps",)))
            E(lambda ho=ho: em.op("act", lambda e: e.activation(
                self.mmt[:, ho:ho + NH], mm_ps[:], AF.Tanh),
                reads=("mm_ps",), writes=("mmt",)))
            if hf == 0:
                E(lambda: self._load_w2(3))
            th_l += self._mix_channel(3, "maa_w", hf, xxh_dst=xwh,
                                      mm_half=hf, m_ps=m_ps, mtf=mtf,
                                      mtb=mtb, full_dst=None, defer=True)
            for kc in range(NF):
                E(lambda kc=kc: em.op("pe", lambda e: e.matmul(
                    u_ps[:], self.tdw1b[:, kc * 64:(kc + 1) * 64],
                    xwh[:, kc * NH:(kc + 1) * NH],
                    start=(kc == 0), stop=(kc == NF - 1)),
                    reads=("tdw1b", "xwh"), writes=("u_ps",)))
            E(lambda ho=ho: em.op("act", lambda e: e.activation(
                self.ubuf[:, ho:ho + NH], u_ps[:], AF.Tanh),
                reads=("u_ps",), writes=("ubuf",)))
            return th_l

        def rchan(hf):
            return self._mix_channel(0, "maa_r", hf, xxh_dst=None, mm_half=hf,
                                     m_ps=m_psr, mtf=mtf, mtb=mtf,
                                     full_dst=self.xch["r"], defer=True,
                                     mpn="m_psr", mbn="mtf", w2t=w2r,
                                     w2n="w2r")

        for t in pass1_thunks(0):
            t()
        self._riffle(pass1_thunks(1), rchan(0))
        for t in rchan(1):
            t()
        self.psum_free("mm_ps", "u_ps")
        self.psum_free("m_psr0", "m_psr1")
        em.barrier()
        ar.free("xxh", "xwh", "w1b", "tdw1b", "w2r", "wld0", "wld1", "wld2",
                "wld3", "wbf0", "wbf1", "wbf2", "wbf3")
        self.m_ps = m_ps

        def chan_thunks(cname, vname, r):
            self.xch[cname] = self.ar.alloc(f"xch{cname}", 128, NF * NT, BF16)
            th_l = [lambda r=r: self._load_w2(r)]
            for hf in range(2):
                th_l += self._mix_channel(r, vname, hf, xxh_dst=None,
                                          mm_half=hf, m_ps=m_ps, mtf=mtf,
                                          mtb=mtb, full_dst=self.xch[cname],
                                          defer=True)
            return th_l

        self.chan_thunks = chan_thunks

    def _chase(self, main, side, lead=2):
        m, n = len(main), len(side)
        jn = 0
        for i, t in enumerate(main):
            while jn < n and (jn - lead) * m < i * n:
                side[jn]()
                jn += 1
            t()
        while jn < n:
            side[jn]()
            jn += 1

    def _riffle(self, main, side):
        jn, n = 0, len(side)
        for i, t in enumerate(main):
            t()
            while jn < n and jn * len(main) < (i + 1) * n:
                side[jn]()
                jn += 1
        while jn < n:
            side[jn]()
            jn += 1

    def _load_w2(self, r):
        em = self.em
        em.dma("pool", "d_wsm", self.w2cur,
               self.w2pad[r * 128:(r + 1) * 128, :], writes=("w2cur",))

    def _mix_channel(self, r, vname, hf, xxh_dst, mm_half, m_ps, mtf, mtb,
                     full_dst, defer=False, mpn="m_ps", mbn="mtb",
                     w2t=None, w2n="w2cur"):
        em = self.em
        NT, NH = self.NT, self.NH
        if w2t is None:
            w2t = self.w2cur
        ho = hf * NH
        th_l = []
        E = th_l.append
        for p in range(NF):
            pi = p % 2
            mb = mtb[pi]
            E(lambda r=r, p=p, pi=pi, ho=ho: em.op(
                "pe", lambda e: e.matmul(
                    m_ps[pi][:], w2t[:, p * 128:(p + 1) * 128],
                    self.mmt[:, ho:ho + NH], start=True, stop=True),
                reads=(w2n, "mmt"), writes=(f"{mpn}{pi}",)))
            E(lambda p=p, pi=pi, ho=ho, mb=mb: em.op(
                "dve", lambda e: e.scalar_tensor_tensor(
                    mb, m_ps[pi][:], self.vc(vname, p),
                    self.dxf[:, p * NT + ho:p * NT + ho + NH],
                    ALU.add, ALU.mult),
                reads=(f"{mpn}{pi}", "vec_t", "dxf"), writes=(f"{mbn}{pi}",)))
            if full_dst is not None:
                dst = full_dst[:, p * NT + ho:p * NT + ho + NH]
                wn = "xch" + [k for k, v in self.xch.items()
                              if v is full_dst][0]
            else:
                dst = xxh_dst[:, p * NH:(p + 1) * NH]
                wn = "xwh"
            hb = p * (NT + 1) + ho + 1
            E(lambda pi=pi, dst=dst, hb=hb, mb=mb, wn=wn: em.op(
                ("dve", "pool")[pi], lambda e: e.tensor_tensor(
                    dst, mb, self.hbuf[:, hb:hb + NH], ALU.add),
                reads=(f"{mbn}{pi}", "hbuf"), writes=(wn,)))
        if defer:
            return th_l
        for t in th_l:
            t()

    def _proj(self, w_hbm, w_name, src_name, src_fn, evac_fn, dsem,
              scale_w=False, gsz=4, defer=False, conv_eng="act"):
        em = self.em
        NT, THS, NTH = self.NT, self.THS, self.NTH
        nb = gsz * NTH
        pb = [self.psum(f"pb{i}", [128, THS], F32) for i in range(nb)]
        groups = [list(range(g, min(g + gsz, NF))) for g in range(0, NF, gsz)]
        steps = [(gi, kc) for gi in range(len(groups)) for kc in range(NF)]
        th_l = []
        E = th_l.append

        def load(t):
            gi, kc = steps[t]
            ocs = groups[gi]
            i = kc % 4
            E(lambda gi=gi, kc=kc, i=i, ocs=ocs: em.dma(
                "sp", f"{dsem}{i}", self.wld[i][:, 0:len(ocs) * 128],
                w_hbm[kc * 128:(kc + 1) * 128,
                      ocs[0] * 128:(ocs[0] + len(ocs)) * 128],
                writes=(f"wld{i}",)))

        def conv(kc, ncol):
            i = kc % 4
            if scale_w:
                if conv_eng == "act":
                    E(lambda i=i, ncol=ncol: em.op("act", lambda e: e.mul(
                        self.wbf[i][:, 0:ncol], self.wld[i][:, 0:ncol],
                        self.wsl), reads=(f"wld{i}", "wsl"),
                        writes=(f"wbf{i}",)))
                else:
                    E(lambda i=i, ncol=ncol: em.op(
                        conv_eng, lambda e: e.tensor_scalar(
                            self.wbf[i][:, 0:ncol], self.wld[i][:, 0:ncol],
                            self.wsl, None, ALU.mult),
                        reads=(f"wld{i}", "wsl"), writes=(f"wbf{i}",)))
            elif conv_eng == "act":
                E(lambda i=i, ncol=ncol: em.op("act", lambda e: e.copy(
                    self.wbf[i][:, 0:ncol], self.wld[i][:, 0:ncol]),
                    reads=(f"wld{i}",), writes=(f"wbf{i}",)))
            else:
                E(lambda i=i, ncol=ncol: em.op(conv_eng, lambda e:
                    e.tensor_copy(
                        self.wbf[i][:, 0:ncol], self.wld[i][:, 0:ncol]),
                    reads=(f"wld{i}",), writes=(f"wbf{i}",)))

        for t in range(3):
            load(t)
        for t, (gi, kc) in enumerate(steps):
            if t + 3 < len(steps):
                load(t + 3)
            ocs = groups[gi]
            conv(kc, len(ocs) * 128)
            i = kc % 4
            for j, oc in enumerate(ocs):
                for th in range(NTH):
                    b = j * NTH + th
                    E(lambda i=i, j=j, th=th, kc=kc, b=b: em.op(
                        "pe", lambda e: e.matmul(
                            pb[b][:], self.wbf[i][:, j * 128:(j + 1) * 128],
                            src_fn(kc, th),
                            start=(kc == 0), stop=(kc == NF - 1)),
                        reads=(f"wbf{i}", src_name), writes=(f"pb{b}",)))
            if kc == NF - 1:
                for j, oc in enumerate(ocs):
                    for th in range(NTH):
                        b = j * NTH + th
                        E(lambda oc=oc, th=th, b=b: evac_fn(
                            oc, th, pb[b], f"pb{b}"))
        if defer:
            return th_l, [f"pb{i}" for i in range(nb)]
        for t in th_l:
            t()
        self.psum_free(*[f"pb{i}" for i in range(nb)])

    def ph1_proj(self):
        em, ar = self.em, self.ar
        NT, NTB, THS, NTH = self.NT, self.NTB, self.THS, self.NTH
        self.wld = [ar.alloc(f"wld{i}", 128, 512, F32) for i in range(4)]
        self.wbf = [ar.alloc(f"wbf{i}", 128, 512, BF16) for i in range(4)]

        def src_of(xt, xname):
            return lambda kc, th: xt[:, kc * NT + th * THS:
                                     kc * NT + th * THS + THS]

        qkst = [ar.alloc("qkst0", 128, NT, BF16),
                ar.alloc("qkst1", 128, NT, BF16)]

        def mk_evac(scr, bias_fn, scale):
            def evac(p, th, pt, pn):
                st = qkst[p % 2]
                sn = f"qkst{p % 2}"
                em.op("act", lambda e, st=st, th=th: e.activation(
                    st[:, th * THS:(th + 1) * THS], pt[:],
                    AF.Identity, bias=bias_fn(p), scale=scale),
                    reads=(pn, "qb8", "vec_t"), writes=(sn,))
                if th == NTH - 1:
                    em.dma("act", f"d_qk{p % 2}",
                           scr[p * 128:(p + 1) * 128, :], st,
                           reads=(sn,), writes=())
            return evac

        riffle = self._riffle
        tq, fq = self._proj(self.q_w, "q_w", "xchr",
                            src_of(self.xch["r"], "xchr"),
                            mk_evac(self.qscr, lambda p: self.qb8[:, p:p + 1],
                                    0.125),
                            "d_wq", gsz=3, defer=True)
        riffle(tq, self.chan_thunks("k", "maa_k", 1))
        self.psum_free(*fq)
        em.barrier()
        ar.free("xchr")
        tk, fk = self._proj(self.k_w, "k_w", "xchk",
                            src_of(self.xch["k"], "xchk"),
                            mk_evac(self.kscr, lambda p: self.vc("k_b", p),
                                    1.0),
                            "d_wk", gsz=3, defer=True)
        riffle(tk, self.chan_thunks("v", "maa_v", 2))
        self.psum_free(*fk)
        self.psum_free("m_ps0", "m_ps1")
        em.barrier()
        ar.free("xchk", "qkst0", "qkst1", "mtf0", "mtf1", "mtb0", "mtb1",
                "hbuf", "dxf", "mmt", "w2cur")


    def _vproj_thunks(self, pbv):
        em, ar = self.em, self.ar
        NT, NTB = self.NT, self.NTB
        self.vtm = ar.alloc("vtm", 128, NTB * C, BF16)
        ntb_grp = min(NTB, len(pbv))
        th = []
        E = th.append

        def vload(vq, kc):
            i = kc % 4
            E(lambda vq=vq, kc=kc, i=i: em.dma(
                "sp", f"d_wv{i}", self.wld[i],
                self.v_w[kc * 128:(kc + 1) * 128, vq * 512:(vq + 1) * 512],
                writes=(f"wld{i}",)))

        for tg in range(0, NTB, ntb_grp):
            nb = min(tg + ntb_grp, NTB) - tg
            for i in range(3):
                vload(0, i)
            for vq in range(4):
                for kc in range(NF):
                    nkc = (vq, kc + 3) if kc + 3 < NF else (vq + 1,
                                                            kc + 3 - NF)
                    if nkc[0] < 4:
                        vload(*nkc)
                    i = kc % 4
                    E(lambda i=i: em.op("act", lambda e: e.copy(
                        self.wbf[i], self.wld[i]),
                        reads=(f"wld{i}",), writes=(f"wbf{i}",)))
                    for tb in range(tg, tg + nb):
                        b = tb - tg
                        E(lambda i=i, tb=tb, kc=kc, b=b: em.op(
                            "pe", lambda e: e.matmul(
                                pbv[b][:],
                                self.xch["v"][:, kc * NT + tb * 128:
                                              kc * NT + tb * 128 + 128],
                                self.wbf[i], start=(kc == 0),
                                stop=(kc == NF - 1), skip_group_check=True),
                            reads=(f"wbf{i}", "xchv"), writes=(f"pb{b}",)))
                for tb in range(tg, tg + nb):
                    b = tb - tg
                    E(lambda tb=tb, vq=vq, b=b: em.op(
                        "dve", lambda e: e.tensor_tensor(
                            self.vtm[:, tb * C + vq * 512:
                                     tb * C + vq * 512 + 512],
                            pbv[b][:], self.vbb[:, vq * 512:(vq + 1) * 512],
                            ALU.add),
                        reads=(f"pb{b}", "vbb"), writes=("vtm",)))
        for tb in range(NTB):
            E(lambda tb=tb: em.dma(
                "sp", f"d_vo{tb % 4}", self.vscr[:, tb * C:(tb + 1) * C],
                self.vtm[:, tb * C:(tb + 1) * C],
                reads=("vtm",), writes=("vscr",)))
        return th

    def ph1_rec(self):
        em, ar = self.em, self.ar
        NT, NW, NTB = self.NT, self.NW, self.NTB
        if self.debug:
            print("ph1_rec arena free:", sum(s for _, s in ar.free_list),
                  "bytes in", ar.free_list)
        A = ar.alloc
        self.mB16f = A("mB16f", 128, NT, BF16)
        self.mA16f = A("mA16f", 128, NT, BF16)
        self.mBwf = A("mBwf", 128, NT, BF16)
        em.op("dve", lambda e: e.tensor_copy(
            self.mB16f.rearrange("p (r f) -> p r f", f=16),
            rep(self.masks["mB16"], NT // 16, 16)),
            reads=("mB16",), writes=("mB16f",))
        em.op("dve", lambda e: e.tensor_copy(
            self.mA16f.rearrange("p (r f) -> p r f", f=16),
            rep(self.masks["mA16"], NT // 16, 16)),
            reads=("mA16",), writes=("mA16f",))
        em.op("dve", lambda e: e.tensor_copy(
            self.mBwf.rearrange("p (r f) -> p r f", f=WIN),
            rep(self.masks["mBw"], NT // WIN, WIN)),
            reads=("mBw",), writes=("mBwf",))
        HWIN = min(8, NW)

        sets = []
        for s in range(2):
            S = {"id": s}
            for nm, parts, nel, dt in [
                    ("gt", 128, NT + 16, F32), ("s1", 128, NT, F32),
                    ("s2", 128, NT, F32), ("E116", 128, NT, F32),
                    ("E1w", 128, NT, F32), ("sv", 128, NT, F32),
                    ("kut", 128, NT, BF16), ("Qp", 128, NT, BF16),
                    ("Qw0", 128, NT, BF16),
                    ("Kp", 128, NT, BF16),
                    ("Kc", 128, NT, BF16), ("Kc1", 128, NT, BF16),
                    ("Kc2", 128, NT, BF16),
                    ("Ks0", 128, NT, BF16),
                    ("qgs0", 128, NT, BF16),
                    ("dcol0", 128, NW, F32),
                    ("e16a", 128, NT // 16, F32),
                    ("e16b", 128, NT // 16, F32)]:
                S[nm] = A(f"{nm}_{s}", parts, nel, dt)
                S[nm + "#"] = f"{nm}_{s}"
            for g in range(2):
                for nm in ("qp", "kp"):
                    S[f"{nm}{g}"] = A(f"{nm}_{s}_{g}", 128, NT, BF16)
                    S[f"{nm}{g}#"] = f"{nm}_{s}_{g}"
            em.op("dve", lambda e, S=S: e.memset(
                S["gt"][:, NT:NT + 16], 0.0), writes=(S["gt#"],))
            sets.append(S)

        def issue_qk(g):
            gp = g % 2
            for s in range(2):
                p = 2 * g + s
                S = sets[s]
                em.dma("sp", f"d_qp{s}{gp}", S[f"qp{gp}"],
                       self.qscr[p * 128:(p + 1) * 128, :],
                       reads=("qscr",), writes=(S[f"qp{gp}#"],))
                em.dma("sp", f"d_kp{s}{gp}", S[f"kp{gp}"],
                       self.kscr[p * 128:(p + 1) * 128, :],
                       reads=("kscr",), writes=(S[f"kp{gp}#"],))

        pst = self.psum("ps_rec", [128, 4096], F32)
        ps_dA, ps_dB = pst[0:64, 0:512], pst[0:64, 512:1024]
        ps_aA, ps_aB = pst[0:48, 1024:1408], pst[0:48, 1536:1920]
        ps_gA, ps_gB = pst[0:32, 2048:2432], pst[0:32, 2560:2944]
        ps_o = [pst[0:128, 0:256], pst[0:128, 512:768]]
        ps_tps = [pst[0:128, 1408:1536], pst[0:128, 1920:2048]]
        ps_kt = [pst[0:64, 2048:2304].bitcast(BF16),
                 pst[0:64, 2560:2816].bitcast(BF16)]
        ps_g = [pst[0:128, 3072:3584], pst[0:128, 3584:4096]]
        Vw3 = {}

        def vw_thunks(g):
            pr = g % 2
            th = []
            for s in range(2):
                p = 2 * g + s
                for og in range(2):
                    th.append(lambda s=s, og=og, p=p, pr=pr: em.dma(
                        "sp", "d_vw%d%d" % (s, og), Vw3[(s, pr)][:, og::2, :],
                        self.vscr[og * 64:(og + 1) * 64, :]
                        .rearrange("p (b c) -> p b c", c=C)
                        [:, :, p * 128:(p + 1) * 128],
                        reads=("vscr",), writes=(sets[s][f"Vw{pr}#"],)))
            return th

        def prep_thunks(g):
            gp = g % 2
            ops0 = self._rec_prep_ops(2 * g, sets[0], ps_g[0], "pb6", gp)
            ops1 = self._rec_prep_ops(2 * g + 1, sets[1], ps_g[1], "pb7", gp)
            r = []
            for i in range(max(len(ops0), len(ops1))):
                if i < len(ops0):
                    r.append(ops0[i])
                if i < len(ops1):
                    r.append(ops1[i])
            return r

        issue_qk(0)
        issue_qk(1)
        pbv = [pst[0:128, i * 512:(i + 1) * 512] for i in range(6)]
        self._riffle(self._vproj_thunks(pbv), prep_thunks(0))
        if self.debug:
            self._dump_fm("d_v", lambda p: self.vtm[:, p * NT:(p + 1) * NT],
                          nchunks=NTB * C // NT)
        em.barrier()
        ar.free("xchv", "vtm", "vbb", "wld0", "wld1", "wld2", "wld3",
                "wbf0", "wbf1", "wbf2", "wbf3")
        for s in range(2):
            S = sets[s]
            for nm, parts, nel, dt in [
                    ("Qw1", 128, NT, BF16), ("Ks1", 128, NT, BF16),
                    ("qgs1", 128, NT, BF16), ("dcol1", 128, NW, F32),
                    ("uniA", 64, NT, BF16), ("uniB", 64, NT, BF16),
                    ("kTs", 64, 512, BF16), ("Sb", 128, 128, BF16),
                    ("Vw0", 64, NW * 128, BF16),
                    ("Vw1", 64, NW * 128, BF16)]:
                S[nm] = A(f"{nm}_{s}", parts, nel, dt)
                S[nm + "#"] = f"{nm}_{s}"
        self.yfm = A("yfm", 128, NF * NT, BF16)
        self.etmp = A("etmp", 64, HWIN * 48, BF16)
        for s in range(2):
            em.op("dve", lambda e, s=s: e.memset(
                ps_tps[s][0:64, 64:128], 0.0), writes=(f"pb{2 + s}",))
            em.op("dve", lambda e, s=s: e.memset(
                ps_tps[s][64:128, 0:64], 0.0), writes=(f"pb{2 + s}",))
        for s in range(2):
            for pr in range(2):
                Vw3[(s, pr)] = sets[s][f"Vw{pr}"].rearrange(
                    "p (a c) -> p a c", c=128)

        def window_thunks(g, pp):
            th = []
            for wg in range(0, NW, 4):
                ng = min(4, NW - wg)
                for s in range(2):
                    S = sets[s]
                    for j in range(ng):
                        w = wg + j
                        th.append(lambda S=S, w=w, j=j, s=s, pp=pp: em.op(
                            "pe", lambda e: e.transpose(
                                ps_kt[s][0:64, j * 128:(j + 1) * 128],
                                S[f"Ks{pp}"][:, w * WIN:(w + 1) * WIN],
                                self.ident),
                            reads=(S[f"Ks{pp}#"], "ident"),
                            writes=(f"pb{4 + s}",)))
                    th.append(lambda S=S, ng=ng, s=s: em.op(
                        "act", lambda e: e.copy(
                            S["kTs"][0:64, 0:ng * 128],
                            ps_kt[s][0:64, 0:ng * 128]),
                        reads=(f"pb{4 + s}",), writes=(S["kTs#"],)))
                for j in range(ng):
                    for s in range(2):
                        th += self._rec_window_unit(
                            2 * g + s, sets[s], wg + j, j,
                            ps_o[s], ps_tps[s], s, pp)
                for s in range(2):
                    p = 2 * g + s
                    th.append(lambda p=p, s=s, wg=wg, ng=ng: em.op(
                        "act", lambda e: e.activation(
                            self.yfm[:, p * NT + wg * WIN:
                                     p * NT + (wg + ng) * WIN],
                            ps_o[s][:, 0:ng * WIN], AF.Identity),
                        reads=(f"pb{s}",), writes=("yfm",)))
            for s in range(2):
                S = sets[s]
                p = 2 * g + s
                th.append(lambda S=S, p=p, s=s, pp=pp: em.dma(
                    "act", f"d_qg{s}",
                    self.qgscr[p * 128:(p + 1) * 128, :], S[f"qgs{pp}"],
                    reads=(S[f"qgs{pp}#"],), writes=("qgscr",)))
            return th

        NGRP = NF // 2
        for t in vw_thunks(0):
            t()
        for g in range(NGRP):
            pp = g % 2
            wt = []
            for s in range(2):
                wt += self._rec_scores(sets[s], HWIN, ps_dA, ps_dB, ps_aA,
                                       ps_aB, ps_gA, ps_gB)
            nt = []
            if g + 1 < NGRP:
                nt += vw_thunks(g + 1)
                if g + 2 < NGRP:
                    nt.append(lambda g=g: issue_qk(g + 2))
                nt += prep_thunks(g + 1)
            nsc = len(wt)
            ndma = (8 + 2 * (self._prep_safe - 1)) if nt else 0
            wt += window_thunks(g, pp)
            jn, n = 0, len(nt)
            for i, t in enumerate(wt):
                t()
                if i < nsc:
                    while jn < ndma and jn * nsc < (i + 1) * ndma:
                        nt[jn]()
                        jn += 1
                    continue
                while jn < n and ((jn - ndma) * (len(wt) - nsc)
                                  < (i + 1 - nsc) * (n - ndma)):
                    nt[jn]()
                    jn += 1
            while jn < n:
                nt[jn]()
                jn += 1
            if g == 3:
                em.dma("act", "d_cc0", self.cc_in0[:],
                       self.S_all[:, 0:C // 2], reads=("SA0",),
                       writes=("cc_in0",))
                em.collective("AllGather", ALU.bypass,
                              [list(range(self.n_cores))],
                              [self.cc_in0.opt()], [self.cc_out0.opt()],
                              reads=("cc_in0",), writes=("cc_out0",))
            if g == 5:
                em.dma("act", "d_ccA", self.cc_inA[:],
                       self.S_all[:, C // 2:3 * C // 4], reads=("SA1",),
                       writes=("cc_inA",))
                em.collective("AllGather", ALU.bypass,
                              [list(range(self.n_cores))],
                              [self.cc_inA.opt()], [self.cc_outA.opt()],
                              reads=("cc_inA",), writes=("cc_outA",))
            if g == NGRP - 1:
                em.dma("act", "d_ccD", self.cc_inD[:], self.Dtotb,
                       reads=("Dtotb",), writes=("cc_inD",))
                em.collective("AllGather", ALU.bypass,
                              [list(range(self.n_cores))],
                              [self.cc_inD.opt()], [self.cc_outD.opt()],
                              reads=("cc_inD",), writes=("cc_outD",))
                em.dma("act", "d_ccB", self.cc_inB[:],
                       self.S_all[:, 3 * C // 4:C], reads=("SA1",),
                       writes=("cc_inB",))
                em.collective("AllGather", ALU.bypass,
                              [list(range(self.n_cores))],
                              [self.cc_inB.opt()], [self.cc_outB.opt()],
                              reads=("cc_inB",), writes=("cc_outB",))
        if self.debug:
            self._dump_fm("d_g", lambda p: self.yfm[:, p * NT:(p + 1) * NT])
        em.barrier()
        self.psum_free("ps_rec")
        names = []
        for S in sets:
            names += [S[k] for k in S if k.endswith("#")]
        ar.free(*names)
        ar.free("mB16f", "mA16f", "mBwf", "etmp", "ubuf", "tdw2b")

    def _rec_prep_ops(self, p, S, ps_g, psn, gp):
        em = self.em
        NT, THS, NTH = self.NT, self.THS, self.NTH
        N16, NWR = NT // 16, NT // WIN
        M = self.masks
        gt, s1, s2 = S["gt"], S["s1"], S["s2"]
        gtn, s1n, s2n = S["gt#"], S["s1#"], S["s2#"]
        qp, kp = S[f"qp{gp}"], S[f"kp{gp}"]
        qpn, kpn = S[f"qp{gp}#"], S[f"kp{gp}#"]
        ops = []
        E = ops.append
        for th in range(NTH):
            sl = slice(th * THS, (th + 1) * THS)
            E(lambda sl=sl: em.op("pe", lambda e: e.matmul(
                ps_g[:, 0:THS], self.tdw2b[:, p * 128:(p + 1) * 128],
                self.ubuf[:, sl], start=True, stop=True,
                skip_group_check=True),
                reads=("tdw2b", "ubuf"), writes=(psn,)))
            E(lambda sl=sl: em.op("act", lambda e: e.activation(
                s1[:, sl], ps_g[:, 0:THS], AF.Exp, bias=self.vc("td", p)),
                reads=(psn, "vec_t"), writes=(s1n,)))
            E(lambda sl=sl: em.op("dve", lambda e: e.tensor_scalar(
                s1[:, sl], s1[:, sl], 5.0, None, ALU.min),
                reads=(s1n,), writes=(s1n,)))
            E(lambda sl=sl: em.op("act", lambda e: e.activation(
                gt[:, sl], s1[:, sl], AF.Exp, scale=-1.0),
                reads=(s1n,), writes=(gtn,)))
            E(lambda sl=sl: em.op("act", lambda e: e.activation(
                S["sv"][:, sl], s1[:, sl], AF.Exp),
                reads=(s1n,), writes=(S["sv#"],)))
        E(lambda: em.op("dve", lambda e: e.tensor_scalar(
            s2, gt[:, 0:NT], -1.0, 1.0, ALU.mult, ALU.add),
            reads=(gtn,), writes=(s2n,)))
        E(lambda: em.op("pool", lambda e: e.tensor_tensor(
            S["kut"], kp, s2, ALU.mult),
            reads=(kpn, s2n), writes=(S["kut#"],)))
        E(lambda: em.op("dve", lambda e: e.memset(s2, 0.0), writes=(s2n,)))
        E(lambda: em.op("dve", lambda e: e.tensor_tensor_scan(
            S[f"qgs{gp}"], gt[:, 0:NT], s2, 1.0, ALU.mult, ALU.add),
            reads=(gtn, s2n), writes=(S[f"qgs{gp}#"],)))
        E(lambda: em.op("dve", lambda e: e.tensor_copy(
            self.Dtotb[:, p:p + 1], S[f"qgs{gp}"][:, NT - 1:NT]),
            reads=(S[f"qgs{gp}#"],), writes=("Dtotb",)))
        g3 = gt[:, 0:NT].rearrange("p (r f) -> p r f", f=16)
        gw3 = gt[:, 0:NT].rearrange("p (r f) -> p r f", f=WIN)
        E(lambda: em.op("pool", lambda e: e.tensor_tensor(
            s1.rearrange("p (r f) -> p r f", f=16), g3,
            rep(M["mA16"], N16, 16), ALU.mult),
            reads=(gtn, "mA16"), writes=(s1n,)))
        E(lambda: em.op("pool", lambda e: e.tensor_tensor(
            s2.rearrange("p (r f) -> p r f", f=16), g3,
            rep(M["mB16"], N16, 16), ALU.mult),
            reads=(gtn, "mB16"), writes=(s2n,)))
        E(lambda: em.op("dve", lambda e: e.tensor_tensor_scan(
            S["E116"], s1, s2, 1.0, ALU.mult, ALU.add),
            reads=(s1n, s2n), writes=(S["E116#"],)))
        E(lambda: em.op("pool", lambda e: e.tensor_tensor(
            s1.rearrange("p (r f) -> p r f", f=16),
            gt[:, 1:NT + 1].rearrange("p (r f) -> p r f", f=16),
            rep(M["nE16"], N16, 16), ALU.mult),
            reads=(gtn, "nE16"), writes=(s1n,)))
        self._prep_safe = len(ops)
        E(lambda: em.op("dve", lambda e: e.tensor_tensor_scan(
            S["Kc"][:, ::-1], s1[:, ::-1], self.mB16f, 1.0,
            ALU.mult, ALU.add), reads=(s1n, "mB16f"), writes=(S["Kc#"],)))
        E(lambda: em.op("pool", lambda e: e.tensor_tensor(
            S["Kc"], S["Kc"], S["kut"], ALU.mult),
            reads=(S["kut#"], S["Kc#"]), writes=(S["Kc#"],)))
        N16b = NT // 16
        E(lambda: em.op("dve", lambda e: e.memset(
            S["e16a"][:, N16b - 1:N16b], 0.0), writes=(S["e16a#"],)))
        E(lambda: em.op("dve", lambda e: e.tensor_copy(
            S["e16a"][:, 0:N16b - 1], S["E116"][:, 31::16]),
            reads=(S["E116#"],), writes=(S["e16a#"],)))
        E(lambda: em.op("dve", lambda e: e.memset(
            S["e16b"][:, N16b - 2:N16b], 0.0), writes=(S["e16b#"],)))
        E(lambda: em.op("dve", lambda e: e.tensor_copy(
            S["e16b"][:, 0:N16b - 2], S["E116"][:, 47::16]),
            reads=(S["E116#"],), writes=(S["e16b#"],)))
        Kc3 = S["Kc"].rearrange("p (s f) -> p s f", f=16)
        Kc13 = S["Kc1"].rearrange("p (s f) -> p s f", f=16)
        Kc23 = S["Kc2"].rearrange("p (s f) -> p s f", f=16)
        ea3 = (S["e16a"].rearrange("p (s o) -> p s o", o=1)
               .broadcast_to([128, N16b, 16]))
        eb3 = (S["e16b"].rearrange("p (s o) -> p s o", o=1)
               .broadcast_to([128, N16b, 16]))
        E(lambda: em.op("pool", lambda e: e.tensor_tensor(
            Kc13, Kc3, ea3, ALU.mult),
            reads=(S["Kc#"], S["e16a#"]), writes=(S["Kc1#"],)))
        E(lambda: em.op("pool", lambda e: e.tensor_tensor(
            Kc23, Kc13, eb3, ALU.mult),
            reads=(S["Kc1#"], S["e16b#"]), writes=(S["Kc2#"],)))
        E(lambda: em.op("pool", lambda e: e.tensor_tensor(
            s1.rearrange("p (r f) -> p r f", f=WIN), gw3,
            rep(M["mAw"], NWR, WIN), ALU.mult),
            reads=(gtn, "mAw"), writes=(s1n,)))
        E(lambda: em.op("pool", lambda e: e.tensor_tensor(
            s2.rearrange("p (r f) -> p r f", f=WIN), gw3,
            rep(M["mBw"], NWR, WIN), ALU.mult),
            reads=(gtn, "mBw"), writes=(s2n,)))
        E(lambda: em.op("dve", lambda e: e.tensor_tensor_scan(
            S["E1w"], s1, s2, 1.0, ALU.mult, ALU.add),
            reads=(s1n, s2n), writes=(S["E1w#"],)))
        E(lambda: em.op("pool", lambda e: e.tensor_tensor(
            s1.rearrange("p (r f) -> p r f", f=WIN),
            gt[:, 1:NT + 1].rearrange("p (r f) -> p r f", f=WIN),
            rep(M["nEw"], NWR, WIN), ALU.mult),
            reads=(gtn, "nEw"), writes=(s1n,)))
        E(lambda: em.op("dve", lambda e: e.tensor_tensor_scan(
            S[f"Ks{gp}"][:, ::-1], s1[:, ::-1], self.mBwf, 1.0,
            ALU.mult, ALU.add), reads=(s1n, "mBwf"), writes=(S[f"Ks{gp}#"],)))
        E(lambda: em.op("dve", lambda e: e.tensor_copy(
            S[f"dcol{gp}"], S["E1w"][:, WIN - 1::WIN]),
            reads=(S["E1w#"],), writes=(S[f"dcol{gp}#"],)))
        for dst, dn, a, an in [
                ("Qp", S["Qp#"], S["E116"], S["E116#"]),
                (f"Qw{gp}", S[f"Qw{gp}#"], S["E1w"], S["E1w#"]),
                (f"qgs{gp}", S[f"qgs{gp}#"], S[f"qgs{gp}"],
                 S[f"qgs{gp}#"])]:
            E(lambda dst=dst, dn=dn, a=a, an=an: em.op(
                "dve", lambda e: e.tensor_tensor(S[dst], qp, a, ALU.mult),
                reads=(qpn, an), writes=(dn,)))
        E(lambda: em.op("pool", lambda e: e.tensor_tensor(
            s1.rearrange("p (r f) -> p r f", f=16), 
            S["sv"].rearrange("p (r f) -> p r f", f=16),
            rep(M["mA16"], N16, 16), ALU.mult),
            reads=(S["sv#"], "mA16"), writes=(s1n,)))
        E(lambda: em.op("pool", lambda e: e.tensor_tensor(
            s2.rearrange("p (r f) -> p r f", f=16),
            S["sv"].rearrange("p (r f) -> p r f", f=16),
            rep(M["mB16"], N16, 16), ALU.mult),
            reads=(S["sv#"], "mB16"), writes=(s2n,)))
        E(lambda: em.op("dve", lambda e: e.tensor_tensor_scan(
            S["sv"], s1, s2, 1.0, ALU.mult, ALU.add),
            reads=(s1n, s2n), writes=(S["sv#"],)))
        E(lambda: em.op("pool", lambda e: e.tensor_tensor(
            S["Kp"], S["kut"], S["sv"], ALU.mult),
            reads=(S["kut#"], S["sv#"]), writes=(S["Kp#"],)))
        E(lambda: em.op("pool", lambda e: e.tensor_tensor(
            S[f"Ks{gp}"], S[f"Ks{gp}"], S["kut"], ALU.mult),
            reads=(S["kut#"], S[f"Ks{gp}#"]), writes=(S[f"Ks{gp}#"],)))
        return ops

    def _rec_scores(self, S, HWIN, ps_dA, ps_dB, ps_aA, ps_aB, ps_gA, ps_gB):
        em = self.em
        th_l = []
        _op = em.op
        def op(*a, **k):
            th_l.append(lambda: _op(*a, **k))
        em = type("E", (), {"op": staticmethod(op)})()
        NT, NW = self.NT, self.NW
        Kp, Qp, Kc = S["Kp"], S["Qp"], S["Kc"]
        Kc1, Kc2 = S["Kc1"], S["Kc2"]
        uniA, uniB = S["uniA"], S["uniB"]
        for w0 in range(0, NW, HWIN):
            nw = min(HWIN, NW - w0)
            pc = slice(w0 * WIN, (w0 + nw) * WIN)
            for wl in range(nw):
                w = w0 + wl
                wc = slice(w * WIN, (w + 1) * WIN)
                for hh, psd, psa, psg, dn, an, gn in (
                        (0, ps_dA, ps_aA, ps_gA, "pb0", "pb2", "pb4"),
                        (1, ps_dB, ps_aB, ps_gB, "pb1", "pb3", "pb5")):
                    hsl = slice(hh * 64, hh * 64 + 64)
                    tp = (64 * hh, 0)
                    em.op("pe", lambda e, hsl=hsl, wc=wc, wl=wl, psd=psd,
                          tp=tp: e.matmul(
                              psd[0:64, wl * WIN:(wl + 1) * WIN],
                              Kp[hsl, wc], Qp[hsl, wc],
                              start=True, stop=True, skip_group_check=True,
                              tile_position=tp),
                          reads=(S["Kp#"], S["Qp#"]), writes=(dn,))
                    em.op("pe", lambda e, hsl=hsl, w=w, wl=wl, psa=psa,
                          tp=tp: e.matmul(
                              psa[0:48, wl * 48:(wl + 1) * 48],
                              Kc[hsl, w * WIN:w * WIN + 48],
                              Qp[hsl, w * WIN + 16:(w + 1) * WIN],
                              start=True, stop=True, skip_group_check=True,
                              tile_position=tp),
                          reads=(S["Kc#"], S["Qp#"]), writes=(an,))
                    em.op("pe", lambda e, hsl=hsl, w=w, wl=wl, psg=psg,
                          tp=tp: e.matmul(
                              psg[0:32, wl * 48:wl * 48 + 32],
                              Kc1[hsl, w * WIN:w * WIN + 32],
                              Qp[hsl, w * WIN + 32:(w + 1) * WIN],
                              start=True, stop=True, skip_group_check=True,
                              tile_position=tp),
                          reads=(S["Kc1#"], S["Qp#"]), writes=(gn,))
                    em.op("pe", lambda e, hsl=hsl, w=w, wl=wl, psg=psg,
                          tp=tp: e.matmul(
                              psg[0:16, wl * 48 + 32:wl * 48 + 48],
                              Kc2[hsl, w * WIN:w * WIN + 16],
                              Qp[hsl, w * WIN + 48:(w + 1) * WIN],
                              start=True, stop=True, skip_group_check=True,
                              tile_position=tp),
                          reads=(S["Kc2#"], S["Qp#"]), writes=(gn,))
            et = self.etmp
            for uni, un, psd, psa, psg, dn, an, gn in (
                    (uniA, S["uniA#"], ps_dA, ps_aA, ps_gA,
                     "pb0", "pb2", "pb4"),
                    (uniB, S["uniB#"], ps_dB, ps_aB, ps_gB,
                     "pb1", "pb3", "pb5")):
                em.op("dve", lambda e, uni=uni, psd=psd, nw=nw, pc=pc:
                      e.tensor_tensor(
                          uni[0:64, pc].rearrange("p (r f) -> p r f", f=WIN),
                          psd[0:64, 0:nw * WIN].rearrange(
                              "p (r f) -> p r f", f=WIN),
                          rep(self.diagm[0:64, :], nw, WIN), ALU.mult),
                      reads=(dn, "diagm"), writes=(un,))
                em.op("dve", lambda e, psa=psa, nw=nw: e.tensor_tensor(
                    et[0:48, 0:nw * 48].rearrange("p (r f) -> p r f", f=48),
                    psa[0:48, 0:nw * 48].rearrange("p (r f) -> p r f", f=48),
                    rep(self.adjm[0:48, :], nw, 48), ALU.mult),
                    reads=(an, "adjm"), writes=("etmp",))
                em.op("dve", lambda e, uni=uni, nw=nw, pc=pc: e.tensor_tensor(
                    uni[0:48, pc].rearrange("p (r f) -> p r f", f=WIN)
                    [:, :, 16:WIN],
                    uni[0:48, pc].rearrange("p (r f) -> p r f", f=WIN)
                    [:, :, 16:WIN],
                    et[0:48, 0:nw * 48].rearrange("p (r f) -> p r f", f=48),
                    ALU.add), reads=("etmp",), writes=(un,))
                em.op("dve", lambda e, psg=psg, nw=nw: e.tensor_tensor(
                    et[0:32, 0:nw * 32].rearrange("p (r f) -> p r f", f=32),
                    psg[0:32, 0:nw * 48].rearrange(
                        "p (r f) -> p r f", f=48)[:, :, 0:32],
                    rep(self.g16m[0:32, :], nw, 32), ALU.mult),
                    reads=(gn, "g16m"), writes=("etmp",))
                em.op("dve", lambda e, uni=uni, nw=nw, pc=pc: e.tensor_tensor(
                    uni[0:32, pc].rearrange("p (r f) -> p r f", f=WIN)
                    [:, :, 32:WIN],
                    uni[0:32, pc].rearrange("p (r f) -> p r f", f=WIN)
                    [:, :, 32:WIN],
                    et[0:32, 0:nw * 32].rearrange("p (r f) -> p r f", f=32),
                    ALU.add), reads=("etmp",), writes=(un,))
                em.op("dve", lambda e, uni=uni, psg=psg, nw=nw, pc=pc:
                      e.tensor_tensor(
                          uni[0:16, pc].rearrange("p (r f) -> p r f", f=WIN)
                          [:, :, 48:WIN],
                          uni[0:16, pc].rearrange("p (r f) -> p r f", f=WIN)
                          [:, :, 48:WIN],
                          psg[0:16, 0:nw * 48].rearrange(
                              "p (r f) -> p r f", f=48)[:, :, 32:48],
                          ALU.add), reads=(gn,), writes=(un,))
        return th_l

    def _rec_window_unit(self, p, S, w, j, ps_o, ps_tps, s, pp):
        em = self.em
        wc = slice(w * WIN, (w + 1) * WIN)
        jc = slice(j * WIN, (j + 1) * WIN)
        vb = w * 128
        pon, ptn = f"pb{s}", f"pb{2 + s}"
        Vw, Vwn = S[f"Vw{pp}"], S[f"Vw{pp}#"]
        ops = []
        E = ops.append
        san = f"SA{p // 8}"
        E(lambda: em.op("act", lambda e: e.copy(
            S["Sb"], self.S_all[:, p * 128:(p + 1) * 128]),
            reads=(san,), writes=(S["Sb#"],)))
        for hh in range(2):
            E(lambda hh=hh: em.op("pe", lambda e, j=j, hh=hh: e.matmul(
                ps_tps[64 * hh:64 * hh + 64, 64 * hh:64 * hh + 64],
                S["kTs"][0:64, j * 128 + 64 * hh:j * 128 + 64 * hh + 64],
                Vw[0:64, vb + hh * 64:vb + hh * 64 + 64],
                start=True, stop=True, skip_group_check=True,
                tile_position=(0, 64 * hh)),
                reads=(S["kTs#"], Vwn), writes=(ptn,)))
        E(lambda: em.op("pe", lambda e: e.matmul(
            ps_o[:, jc], S["Sb"], S[f"Qw{pp}"][:, wc],
            start=True, stop=False, skip_group_check=True),
            reads=(S["Sb#"], S[f"Qw{pp}#"]), writes=(pon,)))

        for hh, un in ((0, "uniA"), (1, "uniB")):
            E(lambda hh=hh, un=un: em.op("pe", lambda e, hh=hh, un=un:
                                         e.matmul(
                ps_o[64 * hh:64 * hh + 64, jc],
                Vw[0:64, vb + hh * 64:vb + hh * 64 + 64],
                S[un][0:64, wc],
                start=False, stop=(hh == 1), skip_group_check=True,
                tile_position=(0, 64 * hh)),
                reads=(Vwn, S[un + "#"]), writes=(pon,)))
        E(lambda: em.op("dve", lambda e: e.scalar_tensor_tensor(
            self.S_all[:, p * 128:(p + 1) * 128],
            self.S_all[:, p * 128:(p + 1) * 128],
            S[f"dcol{pp}"][:, w:w + 1], ps_tps[:],
            ALU.mult, ALU.add),
            reads=(ptn, S[f"dcol{pp}#"]), writes=(san,)))
        return ops

    def ph15_cc(self):
        em, ar = self.em, self.ar
        NT = self.NT
        NC = self.n_cores
        sgb = [ar.alloc("sgb0", 128, C, F32), ar.alloc("sgb1", 128, C, F32)]
        Aacc = ar.alloc("Aacc", 128, C, F32)
        Aacp = ar.alloc("Aacp", 128, C, F32)
        dt8 = ar.alloc("dt8", 128, NC * 16, F32)
        fE = ar.alloc("fE", 128, NC * 16, F32)
        suf = ar.alloc("suf", 128, 16, F32)
        qgl = [ar.alloc("qgl0", 128, NT, BF16), ar.alloc("qgl1", 128, NT, BF16)]

        def qgl_load(p):
            em.dma("sp", f"d_qgl{p % 2}", qgl[p % 2],
                   self.qgscr[p * 128:(p + 1) * 128, :],
                   reads=("qgscr",), writes=(f"qgl{p % 2}",))

        qgl_load(0)
        qgl_load(1)
        for r in range(NC):
            em.dma("pool", f"d_dt{r % 2}", dt8[:, r * 16:(r + 1) * 16],
                   self.cc_outD[r * 128:(r + 1) * 128, :],
                   reads=("cc_outD",), writes=("dt8",))
        cm8 = (self.cmask[:, 0:NC].rearrange("p (c o) -> p c o", o=1)
               .broadcast_to([128, NC, 16]))
        d3 = dt8.rearrange("p (c f) -> p c f", f=16)
        em.op("dve", lambda e: e.tensor_scalar(
            dt8, dt8, -1.0, None, ALU.add), reads=("dt8",), writes=("dt8",))
        em.op("dve", lambda e: e.tensor_tensor(d3, d3, cm8, ALU.mult),
              reads=("dt8", "cmask"), writes=("dt8",))
        em.op("dve", lambda e: e.tensor_scalar(
            dt8, dt8, 1.0, None, ALU.add), reads=("dt8",), writes=("dt8",))
        for r in range(NC - 1, -1, -1):
            if r == NC - 1:
                em.op("dve", lambda e: e.memset(suf, 1.0), writes=("suf",))
            else:
                em.op("dve", lambda e, r=r: e.tensor_tensor(
                    suf, suf, dt8[:, (r + 1) * 16:(r + 2) * 16], ALU.mult),
                    reads=("suf", "dt8"), writes=("suf",))
            em.op("dve", lambda e, r=r: e.tensor_scalar(
                fE[:, r * 16:(r + 1) * 16], suf, self.cmask[:, r:r + 1],
                None, ALU.mult), reads=("suf", "cmask"), writes=("fE",))
        engs = ["dve", "dve", "pool", "dve", "pool", "dve", "pool", "dve"]
        acc = {"dve": (Aacc, "Aacc"), "pool": (Aacp, "Aacp")}
        first = {}
        for hf, c0, wc_, cc_o, cn, hn in (
                (0, 0, C // 2, self.cc_out0, "cc_out0", "l"),
                (1, C // 2, C // 4, self.cc_outA, "cc_outA", "a"),
                (2, 3 * C // 4, C // 4, self.cc_outB, "cc_outB", "b")):
            npr = wc_ // 128
            for r in range(NC):
                pi = r % 2
                bn = f"sgb{pi}{hn}"
                sgh = sgb[pi][:, c0:c0 + wc_]
                em.dma("sp", f"d_sg{hn}{pi}", sgh,
                       cc_o[r * 128:(r + 1) * 128, :],
                       reads=(cn,), writes=(bn,))
                e_ = engs[r]
                at, an = acc[e_]
                ah = at[:, c0:c0 + wc_]
                s3 = sgh.rearrange("p (c f) -> p c f", f=128)
                f3 = (fE[:, r * 16 + c0 // 128:r * 16 + c0 // 128 + npr]
                      .rearrange("p (c o) -> p c o", o=1)
                      .broadcast_to([128, npr, 128]))
                a3 = ah.rearrange("p (c f) -> p c f", f=128)
                if (e_, hf) not in first:
                    first[(e_, hf)] = False
                    em.op(e_, lambda e, a3=a3, s3=s3, f3=f3: e.tensor_tensor(
                        a3, s3, f3, ALU.mult),
                        reads=(bn, "fE"), writes=(an,))
                else:
                    em.op(e_, lambda e, s3=s3, f3=f3: e.tensor_tensor(
                        s3, s3, f3, ALU.mult), reads=(bn, "fE"), writes=(bn,))
                    em.op(e_, lambda e, ah=ah, sgh=sgh, an=an, bn=bn:
                          e.tensor_tensor(ah, ah, sgh, ALU.add),
                          reads=(bn, an), writes=(an,))
        if ("pool", 0) in first:
            em.op("dve", lambda e: e.tensor_tensor(Aacc, Aacc, Aacp, ALU.add),
                  reads=("Aacc", "Aacp"), writes=("Aacc",))
        sinb = ar.alloc("sinb", 128, C, BF16)
        em.op("dve", lambda e: e.tensor_copy(sinb, Aacc),
              reads=("Aacc",), writes=("sinb",))
        if self.debug:
            d = ar.alloc("dump", 128, C, F32)
            em.op("dve", lambda e: e.tensor_copy(d, Aacc), reads=("Aacc",),
                  writes=("dump",))
            em.dma("pool", "d_dbg", self.dbg["d_sin"][:], d, reads=("dump",))
            em.barrier()
            ar.free("dump")
        ps_c = [self.psum("ps_c0", [128, self.THS], F32),
                self.psum("ps_c1", [128, self.THS], F32)]
        for p in range(NF):
            pi = p % 2
            for th in range(self.NTH):
                sl = slice(th * self.THS, (th + 1) * self.THS)
                em.op("pe", lambda e, p=p, sl=sl, pi=pi: e.matmul(
                    ps_c[pi][:], sinb[:, p * 128:(p + 1) * 128],
                    qgl[pi][:, sl], start=True, stop=True),
                    reads=("sinb", f"qgl{pi}"), writes=(f"ps_c{pi}",))
                em.op("dve", lambda e, p=p, sl=sl, pi=pi: e.tensor_tensor(
                    self.yfm[:, p * NT + sl.start:p * NT + sl.stop],
                    self.yfm[:, p * NT + sl.start:p * NT + sl.stop],
                    ps_c[pi][:], ALU.add), reads=(f"ps_c{pi}",),
                    writes=("yfm",))
            if p + 2 < NF:
                qgl_load(p + 2)
        self.psum_free("ps_c0", "ps_c1")
        em.barrier()
        ar.free("sgb0", "sgb1", "Aacc", "Aacp", "dt8", "fE", "suf",
                "sinb", "qgl0", "qgl1", "S_all", "Dtotb")
        if self.debug:
            self._dump_fm("d_y", lambda p: self.yfm[:, p * NT:(p + 1) * NT])

    def ph2(self):
        em, ar = self.em, self.ar
        NT, NTB = self.NT, self.NTB
        self.wld = [ar.alloc(f"wld{i}", 128, 512, F32) for i in range(4)]
        self.wbf = [ar.alloc(f"wbf{i}", 128, 512, BF16) for i in range(4)]
        self.ln_open()

        ps_sum = self.psum("ps_sum", [128, NT], F32)
        ps_sq = self.psum("ps_sq", [128, NT], F32)
        self.ln_stats("yfm", lambda p: self.yfm[:, p * NT:(p + 1) * NT],
                      ps_sum, ps_sq, "ps_sum", "ps_sq")
        yln = ar.alloc("yln", 128, NF * NT, BF16)
        ythunks = []
        for p in range(NF):
            st, sn = ((self.stat3, "stat3"), (self.stat4, "stat4"))[p % 2]
            eng = "pool" if p % 4 == 3 else "dve"
            ythunks.append(lambda p=p, st=st, sn=sn, eng=eng: (
                em.op(eng, lambda e: e.tensor_tensor(
                    st, self.yfm[:, p * NT:(p + 1) * NT], self.stat1,
                    ALU.subtract), reads=("yfm", "stat1"), writes=(sn,)),
                em.op(eng, lambda e: e.tensor_tensor(
                    st, st, self.stat2, ALU.mult),
                    reads=(sn, "stat2"), writes=(sn,)),
                em.op("dve", lambda e: e.tensor_scalar(
                    yln[:, p * NT:(p + 1) * NT], st,
                    self.vc("ln_g", p), self.vc("ln_b", p),
                    ALU.mult, ALU.add),
                    reads=(sn, "vec_t"), writes=("yln",))))
        self.psum_free("ps_sum", "ps_sq")

        afm = ar.alloc("afm", 128, NF * NT, BF16)

        def evac_a(p, th, pt, pn):
            em.op("act", lambda e: e.activation(
                afm[:, p * NT + th * self.THS:p * NT + (th + 1) * self.THS],
                pt[:], AF.Identity), reads=(pn,), writes=("afm",))

        to, fo = self._proj(self.o_w, "o_w", "yln",
                            lambda kc, th: yln[:, kc * NT + th * self.THS:
                                               kc * NT + (th + 1) * self.THS],
                            evac_a, "d_wo", conv_eng="dve", defer=True)
        n_p1 = 3 + NF * (2 + 4 * self.NTH) + 4 * self.NTH
        self._chase(to[:n_p1], ythunks)
        for t in to[n_p1:]:
            t()
        self.psum_free(*fo)
        em.barrier()
        ar.free("yfm", "yln")
        ps_sum = self.psum("ps_sum", [128, NT], F32)
        ps_sq = self.psum("ps_sq", [128, NT], F32)
        self.ln_stats("afm", lambda p: afm[:, p * NT:(p + 1) * NT],
                      ps_sum, ps_sq, "ps_sum", "ps_sq")
        x0f2 = ar.alloc("x0f2", 128, NF * NT, BF16)
        for p in range(NF):
            em.dma("sp", f"d_x0u{p % 4}", x0f2[:, p * NT:(p + 1) * NT],
                   self.x0scr[:, p * 128:(p + 1) * 128], reads=("x0scr",),
                   writes=("x0f2",), transpose=True)
        x1 = ar.alloc("x1", 128, NF * NT, BF16)
        for p in range(NF):
            st, sn = ((self.stat3, "stat3"), (self.stat4, "stat4"))[p % 2]
            eng = "pool" if p % 4 == 3 else "dve"
            em.op(eng, lambda e, p=p, st=st: e.tensor_tensor(
                st, afm[:, p * NT:(p + 1) * NT], self.stat1,
                ALU.subtract), reads=("afm", "stat1"), writes=(sn,))
            em.op(eng, lambda e, p=p, st=st: e.tensor_tensor(
                st, st, self.stat2, ALU.mult),
                reads=(sn, "stat2"), writes=(sn,))
            em.op(eng, lambda e, p=p, st=st: e.tensor_tensor(
                x1[:, p * NT:(p + 1) * NT], st,
                x0f2[:, p * NT:(p + 1) * NT], ALU.add),
                reads=(sn, "x0f2"), writes=("x1",))
        self.psum_free("ps_sum", "ps_sq")
        em.barrier()
        ar.free("afm", "x0f2")
        if self.debug:
            self._dump_fm("d_x1", lambda p: x1[:, p * NT:(p + 1) * NT])

        ps_sum = self.psum("ps_sum", [128, NT], F32)
        ps_sq = self.psum("ps_sq", [128, NT], F32)
        self.ln_stats("x1", lambda p: x1[:, p * NT:(p + 1) * NT],
                      ps_sum, ps_sq, "ps_sum", "ps_sq")
        h2 = ar.alloc("h2", 128, NF * NT, F32)
        for p in range(NF):
            st, sn = ((self.stat3, "stat3"), (self.stat4, "stat4"))[p % 2]
            eng = "pool" if p % 4 == 3 else "dve"
            em.op(eng, lambda e, p=p, st=st: e.tensor_tensor(
                st, x1[:, p * NT:(p + 1) * NT], self.stat1,
                ALU.subtract), reads=("x1", "stat1"), writes=(sn,))
            em.op(eng, lambda e, p=p, st=st: e.tensor_tensor(
                h2[:, p * NT:(p + 1) * NT], st, self.stat2, ALU.mult),
                reads=(sn, "stat2"), writes=("h2",))
        self.psum_free("ps_sum", "ps_sq")
        for i in range(4):
            em.dma("sp", f"d_x1o{i % 2}",
                   self.x1scr[:, i * 4 * NT:(i + 1) * 4 * NT],
                   x1[:, i * 4 * NT:(i + 1) * 4 * NT],
                   reads=("x1",), writes=("x1scr",))
        em.barrier()
        ar.free("x1")
        mxc = ar.alloc("mxc", 128, 2, F32)
        mxr = ar.alloc("mxr", 1, 130, F32)
        em.op("dve", lambda e: e.memset(mxc, 0.0), writes=("mxc",))
        for p in range(NF):
            em.op("dve", lambda e, p=p: e.tensor_reduce(
                mxc[:, 1:2], h2[:, p * NT:(p + 1) * NT], AXX, ALU.max,
                apply_absolute_value=True), reads=("h2",), writes=("mxc",))
            em.op("dve", lambda e: e.tensor_tensor(
                mxc[:, 0:1], mxc[:, 0:1], mxc[:, 1:2], ALU.max),
                reads=("mxc",), writes=("mxc",))
        em.dma("pool", "d_mx", self.mxscr[:], mxc[:, 0:1], reads=("mxc",),
               writes=("mxscr",))
        em.dma("pool", "d_mx", mxr[0:1, 0:128],
               self.mxscr.rearrange("p o -> (o p)")
               .rearrange("(o f) -> o f", o=1),
               reads=("mxscr",), writes=("mxr",))
        em.op("dve", lambda e: e.tensor_reduce(
            mxr[0:1, 128:129], mxr[0:1, 0:128], AXX, ALU.max),
            reads=("mxr",), writes=("mxr",))
        mxi = ar.alloc("mxi", 1, 4, F32)
        em.op("dve", lambda e: e.memset(mxi, 0.0), writes=("mxi",))
        em.op("dve", lambda e: e.tensor_copy(mxi[0:1, 0:1], mxr[0:1, 128:129]),
              reads=("mxr",), writes=("mxi",))
        em.dma("pool", "d_mxs", self.mx_in[:], mxi, reads=("mxi",),
               writes=("mx_in",))
        em.collective("AllReduce", ALU.max, [list(range(self.n_cores))],
                      [self.mx_in.opt()], [self.mx_out.opt()],
                      reads=("mx_in",), writes=("mx_out",))
        em.dma("sp", "d_mxl", mxi, self.mx_out[:], reads=("mx_out",),
               writes=("mxi",))
        em.op("dve", lambda e: e.tensor_scalar(
            mxi[0:1, 1:2], mxi[0:1, 0:1], 1.0 / FP8_MAX, SCALE_EPS,
            ALU.mult, ALU.max), reads=("mxi",), writes=("mxi",))
        em.op("dve", lambda e: e.reciprocal(mxi[0:1, 2:3], mxi[0:1, 1:2]),
              reads=("mxi",), writes=("mxi",))
        onesf = ar.alloc("onesf", 1, 128, F32)
        invc = ar.alloc("invc", 128, 1, F32)
        ps_c = self.psum("ps_c", [128, 16], F32)
        em.op("dve", lambda e: e.memset(onesf, 1.0), writes=("onesf",))
        em.op("pe", lambda e: e.matmul(ps_c[:, 0:1], onesf, mxi[0:1, 2:3],
                                       start=True, stop=True),
              reads=("onesf", "mxi"), writes=("ps_c",))
        em.op("dve", lambda e: e.tensor_copy(invc, ps_c[:, 0:1]),
              reads=("ps_c",), writes=("invc",))
        self.psum_free("ps_c")
        xq = ar.alloc("xq", 128, NF * NT, BF16)
        qs1 = [ar.alloc("qs1a", 128, NT, F32), ar.alloc("qs1b", 128, NT, F32)]
        qs2 = [ar.alloc("qs2a", 128, NT, F32), ar.alloc("qs2b", 128, NT, F32)]
        qs3 = [ar.alloc("qs3a", 128, NT, F32), ar.alloc("qs3b", 128, NT, F32)]
        qthunks = []
        for p in range(NF):
            sl = slice(p * NT, (p + 1) * NT)
            pi = p % 2
            q1, q2, q3 = qs1[pi], qs2[pi], qs3[pi]
            n1, n2, n3 = f"qs1{'ab'[pi]}", f"qs2{'ab'[pi]}", f"qs3{'ab'[pi]}"
            eng = "pool" if p % 2 == 1 else "dve"
            qthunks.append(lambda sl=sl, q1=q1, q2=q2, q3=q3, n1=n1, n2=n2,
                           n3=n3, eng=eng: (
                em.op("dve", lambda e: e.tensor_scalar(
                    q1, h2[:, sl], invc, None, ALU.mult),
                    reads=("h2", "invc"), writes=(n1,)),
                em.op("act", lambda e: e.activation(q2, q1, AF.Abs),
                      reads=(n1,), writes=(n2,)),
                em.op("act", lambda e: e.activation(q2, q2, AF.Ln),
                      reads=(n2,), writes=(n2,)),
                em.op("dve", lambda e: e.tensor_scalar(
                    q2, q2, 1.0 / LN2, -0.5, ALU.mult, ALU.add),
                    reads=(n2,), writes=(n2,)),
                em.op("dve", lambda e: e.tensor_scalar(
                    q2, q2, RNE_C, None, ALU.add),
                    reads=(n2,), writes=(n2,)),
                em.op("dve", lambda e: e.tensor_scalar(
                    q2, q2, -RNE_C, -6.0, ALU.add, ALU.max),
                    reads=(n2,), writes=(n2,)),
                em.op("act", lambda e: e.activation(
                    q3, q2, AF.Exp, scale=LN2, bias=self.cbias[:, 1:2]),
                    reads=(n2, "cbias"), writes=(n3,)),
                em.op(eng, lambda e: e.tensor_tensor(
                    q1, q1, q3, ALU.add), reads=(n3,), writes=(n1,)),
                em.op(eng, lambda e: e.tensor_tensor(
                    xq[:, sl], q1, q3, ALU.subtract),
                    reads=(n1, n3), writes=("xq",))))

        ffm = ar.alloc("ffm", 128, NF * NT, BF16)

        def evac_f(p, th, pt, pn):
            em.op("act", lambda e: e.activation(
                ffm[:, p * NT + th * self.THS:p * NT + (th + 1) * self.THS],
                pt[:], AF.Identity, bias=self.vc("f_b", p)),
                reads=(pn, "vec_t"), writes=("ffm",))

        tf, ff = self._proj(self.ffn_wT, "ffn_wT", "xq",
                            lambda kc, th: xq[:, kc * NT + th * self.THS:
                                              kc * NT + (th + 1) * self.THS],
                            evac_f, "d_wf", scale_w=True, conv_eng="dve",
                            defer=True)
        n_p1 = 3 + NF * (2 + 4 * self.NTH) + 4 * self.NTH
        self._chase(tf[:n_p1], qthunks)
        for t in tf[n_p1:]:
            t()
        self.psum_free(*ff)
        em.barrier()
        ar.free("h2", "qs1a", "qs1b", "qs2a", "qs2b", "qs3a", "qs3b", "mxc",
                "mxr", "mxi", "onesf", "invc")
        if self.debug:
            self._dump_fm("d_xq", lambda p: xq[:, p * NT:(p + 1) * NT])
        ar.free("xq")
        ps_sum = self.psum("ps_sum", [128, NT], F32)
        ps_sq = self.psum("ps_sq", [128, NT], F32)
        self.ln_stats("ffm", lambda p: ffm[:, p * NT:(p + 1) * NT],
                      ps_sum, ps_sq, "ps_sum", "ps_sq")
        outf2 = [ar.alloc("outf0", 128, NT, BF16),
                 ar.alloc("outf1", 128, NT, BF16)]
        x1 = ar.alloc("x1b", 128, NF * NT, BF16)
        for i in range(4):
            em.dma("sp", f"d_x1i{i % 2}",
                   x1[:, i * 4 * NT:(i + 1) * 4 * NT],
                   self.x1scr[:, i * 4 * NT:(i + 1) * 4 * NT],
                   reads=("x1scr",), writes=("x1b",))
        stok = [ar.alloc(f"stok{tb}", 128, C, BF16) for tb in range(NTB)]
        ps_tf = [self.psum("ps_t0", [128, NTB * 64], F32),
                 self.psum("ps_t1", [128, NTB * 64], F32)]
        ps_t = [t.bitcast(BF16) for t in ps_tf]
        for p in range(NF):
            st, sn = ((self.stat3, "stat3"), (self.stat4, "stat4"))[p % 2]
            eng = "pool" if p % 4 == 3 else "dve"
            em.op(eng, lambda e, p=p, st=st: e.tensor_tensor(
                st, ffm[:, p * NT:(p + 1) * NT], self.stat1,
                ALU.subtract), reads=("ffm", "stat1"), writes=(sn,))
            em.op(eng, lambda e, p=p, st=st: e.tensor_tensor(
                st, st, self.stat2, ALU.mult),
                reads=(sn, "stat2"), writes=(sn,))
            outf = outf2[p % 2]
            ofn = f"outf{p % 2}"
            em.op(eng, lambda e, p=p, st=st, outf=outf: e.tensor_tensor(
                outf, st, x1[:, p * NT:(p + 1) * NT], ALU.add),
                reads=(sn, "x1b"), writes=(ofn,))
            pi = p % 2
            ptn = f"ps_t{pi}"
            for tb in range(NTB):
                em.op("pe", lambda e, pi=pi, tb=tb, outf=outf: e.transpose(
                    ps_t[pi][:, tb * 128:(tb + 1) * 128],
                    outf[:, tb * 128:(tb + 1) * 128], self.ident),
                    reads=(ofn, "ident"), writes=(ptn,))
            for tb in range(NTB):
                em.op("dve", lambda e, pi=pi, tb=tb, p=p: e.tensor_copy(
                    stok[tb][:, p * 128:(p + 1) * 128],
                    ps_t[pi][:, tb * 128:(tb + 1) * 128]),
                    reads=(ptn,), writes=(f"stok{tb}",))
        outs = []
        for tb in range(NTB):
            sem = f"d_out{tb % 4}"
            em.dma("pool", sem, self.out_t[tb * 128:(tb + 1) * 128, :],
                   stok[tb], reads=(f"stok{tb}",), writes=("out_t",))
            outs.append(sem)
        self.psum_free("ps_t0", "ps_t1")
        self.psum_free("ps_sum", "ps_sq")
        self.em._emit_waits(
            "pool", [(s, self.em.ctr[s]) for s in set(outs)])


def _host_consts():
    c = {}
    c["ident"] = np.eye(128, dtype=ml_dtypes.bfloat16)
    c["ones"] = np.ones((128, 128), dtype=ml_dtypes.bfloat16)

    def mk(n, fn):
        j = np.arange(n)
        return np.tile(fn(j).astype(np.float32), (128, 1)).astype(
            ml_dtypes.bfloat16)

    c["mA16"] = mk(16, lambda j: j != 0)
    c["mB16"] = mk(16, lambda j: j == 0)
    c["mE16"] = mk(16, lambda j: j == 15)
    c["nE16"] = mk(16, lambda j: j != 15)
    c["mAw"] = mk(WIN, lambda j: j != 0)
    c["mBw"] = mk(WIN, lambda j: j == 0)
    c["mEw"] = mk(WIN, lambda j: j == WIN - 1)
    c["nEw"] = mk(WIN, lambda j: j != WIN - 1)
    dm = np.zeros((128, WIN), dtype=np.float32)
    r = np.arange(64)[:, None]
    cc = np.arange(WIN)[None, :]
    tri = ((r // 16 == cc // 16) & (r % 16 <= cc % 16)).astype(np.float32)
    dm[0:64, :] = tri
    dm[64:128, :] = tri
    c["diagm"] = dm.astype(ml_dtypes.bfloat16)
    am = np.zeros((128, 48), dtype=np.float32)
    r = np.arange(48)[:, None]
    cc = np.arange(48)[None, :]
    am[0:48, :] = (r // 16 == cc // 16).astype(np.float32)
    c["adjm"] = am.astype(ml_dtypes.bfloat16)
    gm = np.zeros((128, 32), dtype=np.float32)
    r = np.arange(32)[:, None]
    cc = np.arange(32)[None, :]
    gm[0:32, :] = (r // 16 == cc // 16).astype(np.float32)
    c["g16m"] = gm.astype(ml_dtypes.bfloat16)
    cb = np.zeros((128, 4), np.float32)
    cb[:, 0] = LN_EPS
    cb[:, 1] = QC_LN
    c["cbias"] = cb
    return c


def _pack_vecs(d):
    cols = [np.ascontiguousarray(np.asarray(d[n], np.float32).reshape(16, 128).T)
            for n in VORD]
    return np.ascontiguousarray(np.concatenate(cols, axis=1))


def host_inputs(inputs, t_core, n_cores):
    T = t_core * n_cores
    x = np.asarray(inputs["L0_attn_x"], dtype=np.float32).reshape(T, C)
    d = {
        "maa_x": inputs["L0_attn_maa_x"], "maa_r": inputs["L0_attn_maa_r"],
        "maa_k": inputs["L0_attn_maa_k"], "maa_v": inputs["L0_attn_maa_v"],
        "maa_w": inputs["L0_attn_maa_w"], "td": inputs["L0_attn_time_decay"],
        "q_b": inputs["L0_attn_q_b"], "k_b": inputs["L0_attn_k_b"],
        "v_b": inputs["L0_attn_v_b"], "ln_g": inputs["L0_attn_ln_g"],
        "ln_b": inputs["L0_attn_ln_b"], "f_b": inputs["L0_ffn_bias"],
    }
    w2 = np.asarray(inputs["L0_attn_maa_w2"], np.float32)
    w2pad = np.zeros((4 * 128, C), np.float32)
    for r in range(4):
        w2pad[r * 128 + r * 32:r * 128 + r * 32 + 32, :] = w2[r]
    base = {
        "q_w": np.ascontiguousarray(inputs["L0_attn_q_w"], np.float32),
        "k_w": np.ascontiguousarray(inputs["L0_attn_k_w"], np.float32),
        "v_w": np.ascontiguousarray(inputs["L0_attn_v_w"], np.float32),
        "o_w": np.ascontiguousarray(inputs["L0_attn_o_w"], np.float32),
        "ffn_wT": np.ascontiguousarray(
            np.asarray(inputs["L0_ffn_weight"], np.float32).T),
        "maa_w1": np.ascontiguousarray(inputs["L0_attn_maa_w1"], np.float32),
        "w2pad": w2pad,
        "td_w1": np.ascontiguousarray(inputs["L0_attn_td_w1"], np.float32),
        "td_w2": np.ascontiguousarray(inputs["L0_attn_td_w2"], np.float32),
        "vecs": _pack_vecs(d),
        "vbx": np.asarray(inputs["L0_attn_v_b"], np.float32).reshape(1, C),
        "ffn_ws": np.asarray(inputs["L0_ffn_weight_scale"],
                             np.float32).reshape(1, 1),
        **_host_consts(),
    }
    in_maps = []
    for cid in range(n_cores):
        t0 = cid * t_core
        m = dict(base)
        m["x_in"] = np.ascontiguousarray(x[t0:t0 + t_core])
        m["xprev"] = (np.zeros((1, C), np.float32) if cid == 0
                      else np.ascontiguousarray(x[t0 - 1:t0]))
        cm = np.zeros((128, 8), np.float32)
        cm[:, :cid] = 1.0
        m["cmask"] = cm
        in_maps.append(m)
    return in_maps


def kernel(**inputs):
    n_cores = 8
    t_core = 8192 // n_cores
    in_maps = host_inputs(inputs, t_core, n_cores)
    nc = build_program(t_core, n_cores, debug=False)
    res = run_bass_kernel_spmd(nc, in_maps, core_ids=list(range(n_cores)))
    out = np.concatenate([np.asarray(r["out"]) for r in res.results], axis=0)
    return np.ascontiguousarray(out.reshape(1, 8192, C), dtype=np.float32)
